# version 17
# baseline (speedup 1.0000x reference)
import contextlib
import math
import numpy as np
import ml_dtypes
import concourse.bass as bass
import concourse.mybir as mybir
from concourse.bass_utils import run_bass_kernel_spmd

F32 = mybir.dt.float32
BF16 = mybir.dt.bfloat16
AF = mybir.ActivationFunctionType
OP = mybir.AluOpType
AX = mybir.AxisListType

B, S, HID = 4, 2048, 2048
NH, D = 16, 128
HPC = 8
TOK = 1024
NTT = TOK // 128
GPC = 2
FFH = 5120
GN = 16.0
EPS = 1e-5
NEG = -1.0e30
N_SUBLAYERS = 4

CONST_VALS = [1e-5, -1.0, 1.0 / 16.0, -1.0 / 16.0, 0.5, 0.7,
              1.0 / (5120.0 * 2048.0), 0.0, 1.0, 2.0]

PAIRS = [[0, 1], [2, 3], [4, 5], [6, 7]]
ALL8 = [list(range(8))]
ENGS = ("tensor", "vector", "scalar", "sync", "gpsimd")


def _np_consts():
    c = {}
    t = np.arange(S, dtype=np.float64)
    inv_a = 1.0 / (10000.0 ** (np.arange(0, 32, 2, dtype=np.float64) / 32.0))
    fa = np.outer(t, inv_a)
    cos_a = np.zeros((32, S)); sin_a = np.zeros((32, S))
    for r in range(16):
        cos_a[r] = np.cos(fa[:, r]); cos_a[r + 16] = np.cos(fa[:, r])
        sin_a[r] = -np.sin(fa[:, r]); sin_a[r + 16] = np.sin(fa[:, r])
    c["cos_a"] = cos_a.astype(np.float32)
    c["sin_a"] = sin_a.astype(np.float32)

    inv_b = 1.0 / (10000.0 ** (np.arange(0, D, 2, dtype=np.float64) / D))
    fb = np.outer(t, inv_b)
    cos_b = np.zeros((D, S)); sin_b = np.zeros((D, S))
    for r in range(64):
        cos_b[r] = np.cos(fb[:, r]); cos_b[r + 64] = np.cos(fb[:, r])
        sin_b[r] = -np.sin(fb[:, r]); sin_b[r + 64] = np.sin(fb[:, r])
    sc = 1.0 / math.sqrt(D)
    c["cos_bq"] = (cos_b * sc).astype(np.float32)
    c["sin_bq"] = (sin_b * sc).astype(np.float32)
    c["cos_bk"] = cos_b.astype(np.float32)
    c["sin_bk"] = sin_b.astype(np.float32)

    p16 = np.zeros((128, 32), dtype=np.float32)
    for r in range(32):
        p16[r ^ 16, r] = 1.0
    c["perm16"] = p16.astype(ml_dtypes.bfloat16)
    p64 = np.zeros((128, 128), dtype=np.float32)
    for r in range(128):
        p64[r ^ 64, r] = 1.0
    c["perm64"] = p64.astype(ml_dtypes.bfloat16)
    c["ident"] = np.eye(128, dtype=np.float32).astype(ml_dtypes.bfloat16)

    m = np.zeros((128, 4 * 512), dtype=np.float32)
    for off in range(4):
        for i in range(128):
            m[i, off * 512 + off * 128 + i + 1: (off + 1) * 512] = NEG
    c["cmask"] = m.astype(ml_dtypes.bfloat16)
    c["trimask"] = np.triu(np.ones((128, 128), dtype=np.float32))
    cv = np.zeros((128, 16), dtype=np.float32)
    for i, v in enumerate(CONST_VALS):
        cv[:, i] = np.float32(v)
    c["constv"] = cv
    return c


def build_nc():
    nc = bass.Bass("TRN2", target_bir_lowering=False, debug=False,
                   detect_race_conditions=False)

    di = {}

    def dram_in(name, shape, dtype=F32):
        di[name] = nc.dram_tensor(name, list(shape), dtype,
                                  kind="ExternalInput").ap()
        return di[name]

    x0 = dram_in("x0", (TOK, HID))
    wqk = dram_in("wqk", (HID, HPC * 256))
    wv_a = dram_in("wv_a", (HID, HPC * 128))
    b_q = dram_in("b_q", (HPC * 128, 1))
    b_k = dram_in("b_k", (HPC * 128, 1))
    b_v = dram_in("b_v", (HPC * 128, 1))
    w_out = dram_in("w_out", (HID, HID))
    b_out = dram_in("b_out", (HID, 1))
    wq_b = dram_in("wq_b", (HID, HPC * 128))
    wk_b = dram_in("wk_b", (HID, GPC * 128))
    wv_b = dram_in("wv_b", (HID, GPC * 128))
    wo_b = dram_in("wo_b", (HID, HID))
    wfc_sh = dram_in("wfc_sh", (HID // 8, FFH))
    wpj_sh = dram_in("wpj_sh", (FFH // 8, HID))
    cos_a = dram_in("cos_a", (32, S))
    sin_a = dram_in("sin_a", (32, S))
    cos_bq = dram_in("cos_bq", (D, S))
    sin_bq = dram_in("sin_bq", (D, S))
    cos_bk = dram_in("cos_bk", (D, S))
    sin_bk = dram_in("sin_bk", (D, S))
    perm16_i = dram_in("perm16", (128, 32), BF16)
    perm64_i = dram_in("perm64", (128, 128), BF16)
    ident_i = dram_in("ident", (128, 128), BF16)
    cmask_i = dram_in("cmask", (128, 4 * 512), BF16)
    trimask_i = dram_in("trimask", (128, 128))
    constv_i = dram_in("constv", (128, 16))

    out_t = nc.dram_tensor("out", [TOK, HID], F32, kind="ExternalOutput").ap()

    ln_mine = nc.dram_tensor("ln_mine", [TOK, HID], BF16).ap()
    ln_all = nc.dram_tensor("ln_all", [2 * TOK, HID], BF16).ap()
    yt_mine = nc.dram_tensor("yt_mine", [2 * TOK, TOK], BF16).ap()
    yt_all_t = nc.dram_tensor("yt_all", [4 * TOK + 128, TOK], BF16)
    yt_all = yt_all_t.ap()[0:4 * TOK, :]
    yt_all_full = yt_all_t.ap()
    qt_sp = nc.dram_tensor("qt_sp", [HPC * 128, S], BF16).ap()
    kt_sp = nc.dram_tensor("kt_sp", [HPC * 128, S], BF16).ap()
    absr_m = nc.dram_tensor("absr_m", [1, 16], F32).ap()
    absr_s = nc.dram_tensor("absr_s", [1, 16], F32).ap()
    wfcq_m = nc.dram_tensor("wfcq_m", [HID // 8, FFH], BF16).ap()
    wfcq_all = nc.dram_tensor("wfcq_all", [HID, FFH], BF16).ap()
    wpjq_m = nc.dram_tensor("wpjq_m", [FFH // 8, HID], BF16).ap()
    wpjq_all = nc.dram_tensor("wpjq_all", [FFH, HID], BF16).ap()
    yt_stage = nc.dram_tensor("yt_stage", [2 * TOK, TOK], BF16).ap()

    groups = []
    last_tok = [None]

    def G(eng, emit, n_dma=0, kind=None, extra_deps=(), chain=True,
          dma_sem=None):
        k = kind or ("dma" if eng in ("sync", "gpsimd") and n_dma > 0 else "eng")
        deps = []
        if chain and last_tok[0] is not None:
            deps.append(last_tok[0])
        deps.extend(extra_deps)
        gi = len(groups)
        groups.append((eng, k, n_dma, deps, emit, dma_sem))
        if chain:
            last_tok[0] = gi
        return gi

    with contextlib.ExitStack() as ES:
        def sb(name, shape, dtype):
            return ES.enter_context(nc.sbuf_tensor(name, list(shape), dtype))

        def psum(name, shape, dtype):
            return ES.enter_context(nc.psum_tensor(name, list(shape), dtype))

        xs = [sb(f"xst{i}", (128, HID), F32) for i in range(NTT)]
        lnb = sb("lnb", (128, HID), BF16)
        AR1 = sb("AR1", (128, 24576), BF16)
        AR2 = sb("AR2", (128, 8192), BF16)
        AR3 = sb("AR3", (128, 6144), F32)
        gsA = sb("gsA", (128, 2048), F32)
        pat = sb("pat", (128, 4096), F32)
        maskT = sb("maskT", (128, 4 * 512), BF16)
        trim = sb("trim", (128, 128), F32)
        wta = sb("wta", (128, 2048), BF16)
        ptT = sb("ptT", (128, 2048), BF16)
        idt = sb("idt", (128, 128), BF16)
        prm16 = sb("prm16", (128, 32), BF16)
        prm64 = sb("prm64", (128, 128), BF16)
        scrq = sb("scrq", (128, 512), BF16)
        scrv = sb("scrv", (128, 512), BF16)
        scr1 = sb("scr1", (128, 128), BF16)
        scr2 = sb("scr2", (128, 128), BF16)
        stat = sb("stat", (128, 48), F32)
        aggr = sb("aggr", (128, 2), F32)
        sc1 = sb("sc1", (128, 16), F32)
        small = sb("small", (128, 16), F32)
        bqh = sb("bqh", (128, HPC), F32)
        bkh = sb("bkh", (128, HPC), F32)
        bvh = sb("bvh", (128, HPC), F32)
        brow = AR3[0:1, 4096:6144]
        onef = sc1[:, 11:12]
        cvals = sb("cvals", (128, 16), F32)
        for _ci, _cv in enumerate(CONST_VALS):
            nc.const_aps.register(F32, _cv, cvals[:, _ci:_ci + 1])

        psS_t = psum("psS", (128, 2048), F32)
        psS = psS_t.ap() if hasattr(psS_t, "ap") else psS_t
        psE = psum("psE", (128, 512), F32)
        psF = psum("psF", (128, 512), F32)
        psT = psum("psT", (128, 2048), BF16)
        psAD = [psS[:, i * 512:(i + 1) * 512] for i in range(4)]

        def A1(c0, w):
            return AR1[:, c0:c0 + w]

        sems = {e: ES.enter_context(nc.semaphore(f"s_{e}")) for e in ENGS}
        sems["cc"] = ES.enter_context(nc.semaphore("s_cc"))
        for dk in ("lnst0", "lnst1", "pwl0", "pwl1", "psp0", "psp1",
                   "pvl0", "pvl1", "ldt", "tld0", "tld1", "tql0", "tql1",
                   "tql2", "tsp0", "tsp1", "wwl0", "wwl1"):
            sems[dk] = ES.enter_context(nc.semaphore(f"s_{dk}"))


        def ld_consts(e, c):
            c.dma(e, pat[0:32, 0:2048], cos_a[:, :])
            c.dma(e, pat[0:32, 2048:4096], sin_a[:, :])
            c.dma(e, maskT[:, :], cmask_i[:, :])
            c.dma(e, trim[:, :], trimask_i[:, :])
            c.dma(e, idt[:, :], ident_i[:, :])
            c.dma(e, prm16[:, :], perm16_i[:, :])
            c.dma(e, prm64[:, :], perm64_i[:, :])
            c.dma_nc(e, bqh[:, :], b_q.rearrange("(h p) o -> p (h o)", p=128))
            c.dma_nc(e, bkh[:, :], b_k.rearrange("(h p) o -> p (h o)", p=128))
            c.dma_nc(e, bvh[:, :], b_v.rearrange("(h p) o -> p (h o)", p=128))
            c.dma(e, brow, b_out.rearrange("a o -> o a"))
            c.dma(e, cvals[:, :], constv_i[:, :])
        G("sync", ld_consts, n_dma=12)

        def ld_x(e, c):
            for i in range(NTT):
                c.dma(e, xs[i][:, :], x0[i * 128:(i + 1) * 128, :])
        G("sync", ld_x, n_dma=NTT)

        G("vector", lambda e, c: e.memset(AR3[0:1, 0:128], 1.0))
        for k in range(4):
            G("tensor", lambda e, c, k=k: e.matmul(
                psE[:, :], AR3[0:1, 0:128],
                brow[0:1, k * 512:(k + 1) * 512], start=True, stop=True))
            G("vector", lambda e, c, k=k: e.tensor_copy(
                gsA[:, k * 512:(k + 1) * 512], psE[:, :]))
        t_bbc = last_tok[0]

        lnbufs = [lnb, ptT]

        def ln_phase():
            def sums(e, c):
                for i in range(NTT):
                    e.reduce_sum(stat[:, i:i + 1], xs[i][:, :], AX.X)
            G("vector", sums)
            def sqs(e, c):
                for i in range(NTT):
                    e.activation(AR3[:, 0:2048], xs[i][:, :], AF.Square,
                                 accum_out=stat[:, 8 + i:9 + i])
            t_sq = G("scalar", sqs)
            def mv(e, c):
                e.tensor_scalar(stat[:, 16:24], stat[:, 0:8], 1.0 / HID,
                                None, OP.mult)
                e.tensor_scalar(stat[:, 8:16], stat[:, 8:16], 1.0 / HID,
                                None, OP.mult)
                e.drain()
                e.tensor_tensor(stat[:, 24:32], stat[:, 16:24],
                                stat[:, 16:24], OP.mult)
                e.drain()
                e.tensor_tensor(stat[:, 24:32], stat[:, 8:16],
                                stat[:, 24:32], OP.subtract)
            G("vector", mv)
            G("scalar", lambda e, c: e.activation(stat[:, 24:32],
                                                  stat[:, 24:32], AF.Sqrt,
                                                  bias=EPS))
            def inv(e, c):
                e.reciprocal(stat[:, 32:40], stat[:, 24:32])
                e.drain()
                e.tensor_tensor(stat[:, 40:48], stat[:, 16:24],
                                stat[:, 32:40], OP.mult)
                e.drain()
                e.tensor_scalar(stat[:, 40:48], stat[:, 40:48], -1.0, None,
                                OP.mult)
            t_inv = G("vector", inv)
            t_dma = {}
            for i in range(NTT):
                buf = lnbufs[i % 2]
                dep_n = [t_inv] if i < 2 else [t_dma[i - 2]]
                t_n = G("scalar", lambda e, c, i=i, buf=buf: e.activation(
                    buf[:, :], xs[i][:, :], AF.Identity,
                    bias=stat[:, 40 + i:41 + i], scale=stat[:, 32 + i:33 + i]),
                    chain=False, extra_deps=dep_n)
                t_dma[i] = G("gpsimd", lambda e, c, i=i, buf=buf: c.dma(
                    e, ln_mine[i * 128:(i + 1) * 128, :], buf[:, :]), n_dma=1,
                    chain=False, extra_deps=[t_n],
                    dma_sem=f"lnst{i % 2}")
            G("vector", lambda e, c: None,
              extra_deps=[t_dma[NTT - 2], t_dma[NTT - 1]])
            return t_sq

        def ag(bi, bo, grps=PAIRS, op="AllGather", chain=True, extra_deps=()):
            def fn(e, c):
                e.collective_compute(op, OP.add if op == "AllReduce"
                                     else OP.bypass, replica_groups=grps,
                                     ins=[bi.opt()], outs=[bo.opt()],
                                     ).then_inc(sems["cc"], 1)
            return G("gpsimd", fn, kind="cc", chain=chain,
                     extra_deps=extra_deps)

        def ldT(src, tb, n_tok=512):
            def fn(e, c):
                for f in range(16):
                    c.dma_t(e, AR2[:, f * 512:f * 512 + n_tok],
                            src[tb * 512:tb * 512 + n_tok,
                                f * 128:(f + 1) * 128])
            G("sync", fn, n_dma=16, dma_sem="ldt")

        def lnT(f):
            return AR2[:, f * 512:(f + 1) * 512]

        def tern_phase(t_sq, t_bbc):
            tc_last = [None]

            def T(eng, emit, n_dma=0, kind=None, extra=()):
                deps = list(extra)
                if tc_last[0] is not None:
                    deps.append(tc_last[0])
                tok = G(eng, emit, n_dma=n_dma, kind=kind, chain=False,
                        extra_deps=deps)
                tc_last[0] = tok
                return tok

            t_ms = G("vector", lambda e, c: (e.memset(small[:, :], 0.0),
                                             e.memset(onef, 1.0)),
                     chain=False, extra_deps=[t_bbc])
            stripes = []
            for r in range(2):
                for c0, w in ((0, 2048), (2048, 2048), (4096, 1024)):
                    stripes.append(("fc", r, c0, w))
            stripes += [("pj", r, 0, 2048) for r in range(5)]

            t_ld, t_ab = {}, {}
            for i, (kind, r, c0, w) in enumerate(stripes):
                src = wfc_sh if kind == "fc" else wpj_sh
                bz = (i % 2) * 2048
                dep = [t_ab[i - 2]] if i >= 2 else \
                    ([t_sq] if i % 2 == 0 else [t_bbc])
                t_ld[i] = G("sync", lambda e, c, src=src, r=r, c0=c0, w=w,
                            bz=bz: c.dma(
                                e, AR3[:, bz:bz + w],
                                src[r * 128:(r + 1) * 128, c0:c0 + w]),
                            n_dma=1, chain=False, extra_deps=dep,
                            dma_sem=f"tld{i % 2}")
                t_ab[i] = G("scalar", lambda e, c, w=w, bz=bz, i=i:
                            e.activation(AR3[:, 4096:4096 + w],
                                         AR3[:, bz:bz + w],
                                         AF.Abs, accum_out=small[:, i:i + 1]),
                            chain=False, extra_deps=[t_ld[i], t_ms])
            tc_last[0] = t_ab[10]
            T("tensor", lambda e, c: e.matmul(psE[0:1, 0:11], onef,
                                              small[:, 0:11], start=True,
                                              stop=True), extra=[t_bbc])
            toks = {"stripes": stripes, "t_ab": t_ab}

            def emit_mid():
                def colred(e, c):
                    e.tensor_copy(small[0:1, 0:11], psE[0:1, 0:11])
                    e.drain()
                    e.tensor_reduce(small[0:1, 12:13], small[0:1, 0:6], AX.X,
                                    OP.add)
                    e.tensor_reduce(small[0:1, 13:14], small[0:1, 6:11],
                                    AX.X, OP.add)
                T("vector", colred)
                T("gpsimd", lambda e, c: c.dma(e, absr_m[0:1, 0:16],
                                               small[0:1, 0:16]), n_dma=1)
                def arfn(e, c):
                    e.collective_compute("AllReduce", OP.add,
                                         replica_groups=ALL8,
                                         ins=[absr_m.opt()],
                                         outs=[absr_s.opt()],
                                         ).then_inc(sems["cc"], 1)
                T("gpsimd", arfn, kind="cc")
                T("gpsimd", lambda e, c: c.dma(e, small[0:1, 6:8],
                                               absr_s[0:1, 12:14]), n_dma=1)
                def amth(e, c):
                    e.tensor_scalar(small[0:1, 8:9], small[0:1, 6:7],
                                    1.0 / (FFH * HID), 1e-5, OP.mult, OP.max)
                    e.tensor_scalar(small[0:1, 9:10], small[0:1, 7:8],
                                    1.0 / (FFH * HID), 1e-5, OP.mult, OP.max)
                    e.drain()
                    e.tensor_scalar(small[0:1, 10:11], small[0:1, 8:9], 0.7,
                                    None, OP.mult)
                    e.tensor_scalar(small[0:1, 11:12], small[0:1, 9:10], 0.7,
                                    None, OP.mult)
                    e.tensor_scalar(small[0:1, 12:13], small[0:1, 8:9],
                                    small[0:1, 8:9], None, OP.mult)
                    e.drain()
                    e.tensor_scalar(small[0:1, 12:13], small[0:1, 12:13],
                                    small[0:1, 9:10], None, OP.mult)
                T("vector", amth)
                T("vector", lambda e, c: e.memset(AR3[0:1, 128:256], 1.0))
                T("tensor", lambda e, c: e.matmul(
                    psE[:, 0:3], AR3[0:1, 128:256], small[0:1, 10:13],
                    start=True, stop=True))
                toks["ps_free"] = T("vector", lambda e, c: e.tensor_copy(
                    small[:, 13:16], psE[:, 0:3]))
                def nthr(e, c):
                    e.tensor_scalar(sc1[:, 6:7], small[:, 13:14], -1.0, None,
                                    OP.mult)
                    e.tensor_scalar(sc1[:, 7:8], small[:, 14:15], -1.0, None,
                                    OP.mult)
                T("vector", nthr)
            toks["emit_mid"] = emit_mid

            def emit_quant(t_ar2_free):
                t_qld, t_tq, t_sp = {}, {}, {}
                for i, (kind, r, c0, w) in enumerate(stripes):
                    src = wfc_sh if kind == "fc" else wpj_sh
                    dst = wfcq_m if kind == "fc" else wpjq_m
                    thr = small[:, 13:14] if kind == "fc" \
                        else small[:, 14:15]
                    nth = sc1[:, 6:7] if kind == "fc" else sc1[:, 7:8]
                    bz = (i % 3) * 2048
                    sz = (i % 2) * 2048
                    dep = [t_tq[i - 3]] if i >= 3 else \
                        [t_ab[10], toks["ps_free"]]
                    t_qld[i] = G("gpsimd", lambda e, c, src=src, r=r, c0=c0,
                                 w=w, bz=bz: c.dma(
                                     e, AR3[:, bz:bz + w],
                                     src[r * 128:(r + 1) * 128, c0:c0 + w]),
                                 n_dma=1, chain=False, extra_deps=dep,
                                 dma_sem=f"tql{i % 3}")
                    dep_tq = [t_qld[i], tc_last[0]] + \
                        ([t_sp[i - 2]] if i >= 2 else [t_ar2_free])
                    def tq(e, c, w=w, thr=thr, nth=nth, bz=bz, sz=sz):
                        e.tensor_scalar(AR2[:, sz:sz + w], AR3[:, bz:bz + w],
                                        thr, None, OP.is_gt)
                        e.drain()
                        e.tensor_scalar(AR3[:, bz:bz + w], AR3[:, bz:bz + w],
                                        nth, None, OP.is_lt)
                        e.drain()
                        e.tensor_tensor(AR2[:, sz:sz + w], AR2[:, sz:sz + w],
                                        AR3[:, bz:bz + w], OP.subtract)
                    t_tq[i] = G("vector", tq, chain=False, extra_deps=dep_tq)
                    tc_last[0] = t_tq[i]
                    t_sp[i] = G("gpsimd", lambda e, c, dst=dst, r=r, c0=c0,
                                w=w, sz=sz: c.dma(
                                    e, dst[r * 128:(r + 1) * 128, c0:c0 + w],
                                    AR2[:, sz:sz + w]), n_dma=1, chain=False,
                                extra_deps=[t_tq[i]], dma_sem=f"tsp{i % 2}")
                toks["chain"] = t_sp[10]
                toks["sp9"] = t_sp[9]
                toks["ag_fc"] = ag(wfcq_m, wfcq_all, ALL8, chain=False,
                                   extra_deps=[t_sp[10], t_sp[9]])
                toks["ag_pj"] = ag(wpjq_m, wpjq_all, ALL8, chain=False,
                                   extra_deps=[toks["ag_fc"]])
            toks["emit_quant"] = emit_quant
            return toks

        def rope(kind, srcT, dstT, inplace=False):
            if kind == "a":
                prm, nr = prm16, 32
            else:
                prm, nr = prm64, 128
            for nt in range(4):
                sl = slice(nt * 512, (nt + 1) * 512)
                G("tensor", lambda e, c, sl=sl, prm=prm, nr=nr: e.matmul(
                    psF[0:nr, 0:512], prm[:, 0:nr], srcT[:, sl],
                    start=True, stop=True))
                def tts(e, c, sl=sl, nr=nr, inplace=inplace):
                    e.tensor_tensor(AR3[0:nr, 0:512], psF[0:nr, 0:512],
                                    pat[0:nr, 2048 + sl.start:2048 + sl.stop],
                                    OP.mult)
                    e.tensor_tensor(AR3[0:nr, 512:1024], srcT[0:nr, sl],
                                    pat[0:nr, sl], OP.mult)
                    if nr == 32 and not inplace:
                        e.tensor_copy(dstT[:, sl], srcT[:, sl])
                    e.drain()
                    e.tensor_tensor(dstT[0:nr, sl], AR3[0:nr, 0:512],
                                    AR3[0:nr, 512:1024], OP.add)
                G("vector", tts)

        def ld_pat(e, c, cosd, sind):
            c.dma(e, pat[0:cosd.shape[0], 0:2048], cosd[:, :])
            c.dma(e, pat[0:sind.shape[0], 2048:4096], sind[:, :])

        def smax_head(kind, h, ktbuf, ytile):
            entry = last_tok[0]
            toks = {}

            def pbw(qc, w):
                return lnb[:, 0:w] if qc % 2 == 0 else AR2[:, 0:w]

            def pb(qc, b_):
                return pbw(qc, 2048)[:, b_ * 128:(b_ + 1) * 128]

            def mk_smm(qc):
                ktmax = qc // 4 + 1
                def smm(e, c, qc=qc, ktmax=ktmax, ktbuf=ktbuf):
                    for kt in range(ktmax):
                        e.matmul(psAD[kt][:, :],
                                 A1(18432, 2048)[:, qc * 128:(qc + 1) * 128],
                                 ktbuf[:, kt * 512:(kt + 1) * 512],
                                 start=True, stop=True)
                toks[("smm", qc)] = G(
                    "tensor", smm, chain=False,
                    extra_deps=[toks[("ex", qc - 1)]] if qc else [entry])

            mk_smm(0)
            for qc in range(16):
                ktmax = qc // 4 + 1
                wid = 512 * ktmax
                def mx(e, c, qc=qc, ktmax=ktmax, wid=wid):
                    off = qc % 4
                    e.tensor_tensor(psAD[ktmax - 1][:, :],
                                    psAD[ktmax - 1][:, :],
                                    maskT[:, off * 512:(off + 1) * 512],
                                    OP.add)
                    e.drain()
                    e.reduce_max(sc1[:, 4:5], psS[:, 0:wid], AX.X,
                                 negate=True)
                t_mx = G("vector", mx, chain=False,
                         extra_deps=[toks[("smm", qc)]])
                ac = 4 + (qc % 2) * 2
                def ex(e, c, qc=qc, wid=wid, ac=ac):
                    e.activation(pbw(qc, wid), psS[:, 0:wid], AF.Exp,
                                 bias=sc1[:, 4:5],
                                 accum_out=small[:, ac:ac + 1])
                t_ex = G("scalar", ex, chain=False,
                         extra_deps=[t_mx] + ([toks[("tp", qc - 2)]]
                                              if qc >= 2 else []))
                toks[("ex", qc)] = t_ex
                if qc + 1 < 16:
                    mk_smm(qc + 1)
                def nrm(e, c, qc=qc, wid=wid, ac=ac):
                    e.reciprocal(small[:, ac + 1:ac + 2], small[:, ac:ac + 1])
                    e.drain()
                    if kind == "b":
                        e.tensor_scalar(small[:, ac + 1:ac + 2],
                                        small[:, ac + 1:ac + 2], 0.5,
                                        None, OP.mult)
                        e.drain()
                    e.tensor_scalar(pbw(qc, wid), pbw(qc, wid),
                                    small[:, ac + 1:ac + 2], None, OP.mult)
                t_nrm = G("vector", nrm, chain=False, extra_deps=[t_ex])
                nblk = qc + 1
                def tp(e, c, qc=qc, nblk=nblk):
                    for b_ in range(nblk):
                        e.transpose(psT[:, b_ * 128:(b_ + 1) * 128],
                                    pb(qc, b_), idt[:, :])
                t_tp = G("tensor", tp, chain=False,
                         extra_deps=[t_nrm] + ([toks[("pe", qc - 1)]] if qc
                                               else []))
                toks[("tp", qc)] = t_tp
                t_pe = G("vector", lambda e, c, nblk=nblk: e.tensor_copy(
                    ptT[:, 0:nblk * 128], psT[:, 0:nblk * 128]),
                    chain=False, extra_deps=[t_tp])
                toks[("pe", qc)] = t_pe
                def pv(e, c, nblk=nblk, h=h):
                    for b_ in range(nblk):
                        if kind == "a":
                            lhs = A1(b_ * 1024, 1024)[:, h * 128:(h + 1) * 128]
                        else:
                            gg = h // 4
                            lhs = A1(b_ * 256, 256)[:, gg * 128:(gg + 1) * 128]
                        e.matmul(psE[:, 0:128], lhs,
                                 ptT[:, b_ * 128:(b_ + 1) * 128],
                                 start=(b_ == 0), stop=(b_ == nblk - 1))
                t_pv = G("tensor", pv, chain=False,
                         extra_deps=[t_pe] + ([toks[("ye", qc - 1)]] if qc
                                              else []))
                if kind == "a":
                    t_ye = G("vector", lambda e, c, qc=qc, h=h:
                             e.tensor_scalar(
                                 ytile[:, qc * 128:(qc + 1) * 128],
                                 psE[:, 0:128], bvh[:, h:h + 1], None,
                                 OP.add),
                             chain=(qc == 15), extra_deps=[t_pv])
                else:
                    t_ye = G("vector", lambda e, c, qc=qc: e.tensor_copy(
                        ytile[:, qc * 128:(qc + 1) * 128], psE[:, 0:128]),
                        chain=(qc == 15), extra_deps=[t_pv])
                toks[("ye", qc)] = t_ye
        def new_proj_state():
            return {"j": 0, "vt": 0,
                    "last_mm": {0: None, 1: None},
                    "last_ep": {0: None, 1: None},
                    "last_sp": {0: None, 1: None},
                    "last_vmm": {0: None, 1: None},
                    "last_vev": {0: None, 1: None},
                    "last_vld": None,
                    "tail": []}

        def proj_job(pj, tb_tok, wdram, col0, bias_ap, spill, h, tb,
                     direct_dst=None, w_entry=None, extra_mm=()):
            j = pj["j"]; pj["j"] += 1
            par = j % 2
            wbuf = wta if par == 0 else ptT
            sbuf = scrq if par == 0 else scrv
            ps = psE if par == 0 else psF
            dep_wl = ([pj["last_mm"][par]] if pj["last_mm"][par] is not None
                      else ([w_entry] if w_entry is not None else [tb_tok]))
            def wl(e, c, col0=col0, wbuf=wbuf, wdram=wdram):
                c.dma(e, wbuf.rearrange("p (f c) -> p f c", f=16),
                      wdram[:, col0:col0 + 128].rearrange(
                          "(f p) c -> p f c", p=128))
            t_wl = G("gpsimd", wl, n_dma=1,
                     chain=False, extra_deps=dep_wl, dma_sem=f"pwl{par}")
            dep_mm = [t_wl, tb_tok] + list(extra_mm) + (
                [pj["last_ep"][par]] if pj["last_ep"][par] is not None else [])
            def mm(e, c, wbuf=wbuf, ps=ps):
                for f in range(16):
                    e.matmul(ps[:, :], wbuf[:, f * 128:(f + 1) * 128], lnT(f),
                             start=(f == 0), stop=(f == 15))
            t_mm = G("tensor", mm, chain=False, extra_deps=dep_mm)
            pj["last_mm"][par] = t_mm
            if direct_dst is not None:
                t_ep = G("scalar", lambda e, c, ps=ps, d=direct_dst:
                         e.activation(d, ps[:, :], AF.Copy),
                         chain=False, extra_deps=[t_mm])
                pj["last_ep"][par] = t_ep
                pj["tail"].append(t_ep)
            else:
                dep_ep = [t_mm] + (
                    [pj["last_sp"][par]] if pj["last_sp"][par] is not None
                    else [])
                if bias_ap is None:
                    t_ep = G("scalar", lambda e, c, ps=ps, sbuf=sbuf:
                             e.activation(sbuf[:, :], ps[:, :], AF.Copy),
                             chain=False, extra_deps=dep_ep)
                else:
                    t_ep = G("scalar", lambda e, c, ps=ps, sbuf=sbuf,
                             bias_ap=bias_ap: e.activation(
                                 sbuf[:, :], ps[:, :], AF.Identity,
                                 bias=bias_ap), chain=False,
                             extra_deps=dep_ep)
                pj["last_ep"][par] = t_ep
                t_sp = G("gpsimd", lambda e, c, spill=spill, h=h, tb=tb,
                         sbuf=sbuf: c.dma(
                             e, spill[h * 128:(h + 1) * 128,
                                      tb * 512:(tb + 1) * 512], sbuf[:, :]),
                         n_dma=1, chain=False, extra_deps=[t_ep],
                         dma_sem=f"psp{par}")
                pj["last_sp"][par] = t_sp
                pj["tail"].append(t_sp)

        def v_job(pj, tb_tok, wdram, vcol0, tb, tc_dst, w_entry=None):
            vt = pj["vt"]; pj["vt"] += 1
            par = vt % 2
            vbuf = A1(16384 + par * 4096, 4096)
            dep_ld = ([pj["last_vmm"][par]] if pj["last_vmm"][par] is not None
                      else ([w_entry] if w_entry is not None else [tb_tok]))
            def vld(e, c, vbuf=vbuf, vcol0=vcol0, wdram=wdram):
                c.dma(e, vbuf.rearrange("p (f c) -> p f c", f=16),
                      wdram[:, vcol0:vcol0 + 256].rearrange(
                          "(f p) c -> p f c", p=128))
            t_ld = G("gpsimd", vld, n_dma=1,
                     chain=False, extra_deps=dep_ld, dma_sem=f"pvl{par}")
            pj["last_vld"] = t_ld
            prev_ev = pj["last_vev"][0] if (vt % 2 == 1) else \
                pj["last_vev"][1]
            dep_mm = [t_ld, tb_tok] + ([prev_ev] if prev_ev is not None
                                       else [])
            def vmm(e, c, vbuf=vbuf):
                for f in range(16):
                    for tc in range(4):
                        e.matmul(psAD[tc][:, 0:256],
                                 lnT(f)[:, tc * 128:(tc + 1) * 128],
                                 vbuf[:, f * 256:(f + 1) * 256],
                                 start=(f == 0), stop=(f == 15))
            t_mm = G("tensor", vmm, chain=False, extra_deps=dep_mm)
            pj["last_vmm"][par] = t_mm
            def vev(e, c, tc_dst=tc_dst):
                for tc in range(4):
                    e.activation(tc_dst(tc), psAD[tc][:, 0:256], AF.Copy)
            t_ev = G("scalar", vev, chain=False, extra_deps=[t_mm])
            pj["last_vev"][par] = t_ev
            pj["tail"].append(t_ev)
            return t_mm

        def ldT_g(src, tb, deps):
            def fn(e, c):
                for f in range(16):
                    c.dma_t(e, AR2[:, f * 512:f * 512 + 512],
                            src[tb * 512:(tb + 1) * 512,
                                f * 128:(f + 1) * 128])
            return G("sync", fn, n_dma=16, chain=False, extra_deps=deps,
                     dma_sem="ldt")

        def attn_a(tern_toks):
            w_entry = last_tok[0]
            t_ag = ag(ln_mine, ln_all)
            tern_toks["emit_mid"]()
            pj = new_proj_state()
            t_tb = {}
            for tb in range(4):
                t_ldt = ldT_g(ln_all, tb,
                              [t_ag] if tb == 0 else [t_tb[tb - 1]])
                for h in range(HPC):
                    em = [tern_toks["ps_free"]] if (tb == 0 and h == 0) \
                        else ()
                    proj_job(pj, t_ldt, wqk, h * 256, bqh[:, h:h + 1],
                             qt_sp, h, tb, w_entry=w_entry, extra_mm=em)
                    proj_job(pj, t_ldt, wqk, h * 256 + 128, bkh[:, h:h + 1],
                             kt_sp, h, tb, w_entry=w_entry, extra_mm=em)
                last_vmm = None
                for v4 in range(4):
                    last_vmm = v_job(
                        pj, t_ldt, wv_a, v4 * 256, tb,
                        tc_dst=lambda tc, tb=tb, v4=v4:
                            A1((tb * 4 + tc) * 1024, 1024)[:, v4 * 256:
                                                           (v4 + 1) * 256],
                        w_entry=w_entry)
                t_tb[tb] = last_vmm
            tern_toks["emit_quant"](t_tb[3])
            G("vector", lambda e, c: None,
              extra_deps=[t_tb[3], pj["tail"][-1],
                          pj["last_sp"][0], pj["last_sp"][1],
                          tern_toks["chain"], tern_toks["sp9"]])
            for h in range(HPC):
                G("sync", lambda e, c, h=h: c.dma(
                    e, A1(16384, 2048)[:, :], qt_sp[h * 128:(h + 1) * 128, :]),
                  n_dma=1)
                G("sync", lambda e, c, h=h: c.dma(
                    e, A1(20480, 2048)[:, :], kt_sp[h * 128:(h + 1) * 128, :]),
                  n_dma=1)
                rope("a", A1(16384, 2048), A1(18432, 2048))
                rope("a", A1(20480, 2048), A1(20480, 2048), inplace=True)
                smax_head("a", h, A1(20480, 2048), A1(22528, 2048))
                def yst(e, c, h=h):
                    c.dma(e, yt_mine[h * 128:(h + 1) * 128, :],
                          A1(22528, 2048)[:, 0:1024])
                    c.dma(e, yt_mine[1024 + h * 128:1024 + (h + 1) * 128, :],
                          A1(22528, 2048)[:, 1024:2048])
                G("gpsimd", yst, n_dma=2)
            ag(yt_mine, yt_all)
            wo_resid(yt_all_full, w_out, True)
            def badd(e, c):
                for i in range(NTT):
                    e.tensor_tensor(xs[i][:, :], xs[i][:, :], gsA[:, :],
                                    OP.add)
            G("vector", badd)

        _wo_ctr = [0]

        def wo_resid(ytall, wdram, has_wait_pid):
            _wo_ctr[0] += 1
            _rname = f"roff{_wo_ctr[0]}"
            def ystage(e, c):
                pid = nc.partition_id()
                with e.register(_rname) as r:
                    e.reg_mov(r, (pid % 2) * TOK)
                    roff = e.snap(r)
                    for sl in range(2):
                        c.dma(e, yt_stage[sl * TOK:(sl + 1) * TOK, :],
                              ytall[sl * 2048:sl * 2048 + 2 * TOK + 128]
                              [bass.ds(roff, TOK), :])
            G("sync", ystage, n_dma=2)
            def yld(e, c):
                for f in range(16):
                    r0 = (f // 8) * 1024 + (f % 8) * 128
                    c.dma(e, A1(f * 1024, 1024), yt_stage[r0:r0 + 128, :])
            G("sync", yld, n_dma=16)
            entry = last_tok[0]
            wbufs = [AR2, A1(16384, 8192)]
            t_wl, t_mm, t_ra = {}, {}, {}
            for k in range(8):
                thalf, ob = k // 4, k % 4
                buf = wbufs[k % 2]
                dep_wl = [t_mm[k - 2]] if k >= 2 else [entry]
                def wl(e, c, ob=ob, buf=buf):
                    c.dma(e, buf.rearrange("p (f c) -> p f c", f=16),
                          wdram[:, ob * 512:(ob + 1) * 512].rearrange(
                              "(f p) c -> p f c", p=128))
                t_wl[k] = G("gpsimd", wl, n_dma=1,
                            chain=False, extra_deps=dep_wl,
                            dma_sem=f"wwl{k % 2}")
                def mm(e, c, thalf=thalf, buf=buf):
                    for f in range(16):
                        for tc in range(4):
                            e.matmul(psAD[tc][:, :],
                                     A1(f * 1024, 1024)[:, thalf * 512 +
                                                        tc * 128:
                                                        thalf * 512 +
                                                        (tc + 1) * 128],
                                     buf[:, f * 512:(f + 1) * 512],
                                     start=(f == 0), stop=(f == 15))
                t_mm[k] = G("tensor", mm, chain=False,
                            extra_deps=[t_wl[k]] +
                            ([t_ra[k - 1]] if k >= 1 else []))
                def radd(e, c, thalf=thalf, ob=ob):
                    for tc in range(4):
                        i = thalf * 4 + tc
                        e.tensor_tensor(xs[i][:, ob * 512:(ob + 1) * 512],
                                        xs[i][:, ob * 512:(ob + 1) * 512],
                                        psAD[tc][:, :], OP.add)
                t_ra[k] = G("vector", radd, chain=(k == 7),
                            extra_deps=[t_mm[k]])

        def ffn(tern_toks):
            t_ag_fc = tern_toks.get("ag_fc")
            t_ag_pj = tern_toks.get("ag_pj")
            for wave in range(2):
                ldT(ln_mine, wave)
                entry1 = last_tok[0]
                wb1 = [A1(20480, 2048), A1(22528, 2048)]
                t_swl, t_mm1, t_rsq = {}, {}, {}
                for hc in range(40):
                    par = hc % 2
                    dep = [t_mm1[hc - 2]] if hc >= 2 else [entry1]
                    if t_ag_fc is not None and hc < 2:
                        dep = dep + [t_ag_fc]
                    def swl(e, c, hc=hc, buf=wb1[par]):
                        c.dma(e, buf.rearrange("p (f h) -> p f h", f=16),
                              wfcq_all[:, hc * 128:(hc + 1) * 128].rearrange(
                                  "(f p) h -> p f h", p=128))
                    t_swl[hc] = G("sync" if par == 0 else "gpsimd", swl,
                                  n_dma=1, chain=False, extra_deps=dep)
                    def mm(e, c, buf=wb1[par], par=par):
                        ps = psAD[par]
                        for f in range(16):
                            e.matmul(ps[:, :], buf[:, f * 128:(f + 1) * 128],
                                     lnT(f), start=(f == 0), stop=(f == 15))
                    t_mm1[hc] = G("tensor", mm, chain=False,
                                  extra_deps=[t_swl[hc]] +
                                  ([t_rsq[hc - 2]] if hc >= 2 else []))
                    def rsq(e, c, hc=hc, par=par):
                        ps = psAD[par]
                        scr = AR3[:, par * 512:(par + 1) * 512]
                        e.tensor_scalar(scr, ps[:, :], 0.0, None, OP.max)
                        e.drain()
                        e.tensor_tensor(A1(hc * 512, 512), scr, ps[:, :],
                                        OP.mult)
                    t_rsq[hc] = G("vector", rsq, chain=(hc == 39),
                                  extra_deps=[t_mm1[hc]])
                for ob in range(4):
                    ldk, mmk = {}, {}
                    entry2 = last_tok[0]
                    for hb in range(10):
                        par = hb % 2
                        def rwl(e, c, ob=ob, hb=hb, par=par):
                            for j in range(4):
                                hh = hb * 4 + j
                                c.dma(e, AR2[:, par * 4096 + j * 512:
                                             par * 4096 + (j + 1) * 512],
                                      wpjq_all[hh * 128:(hh + 1) * 128,
                                               ob * 512:(ob + 1) * 512])
                        dep = [mmk[hb - 2]] if hb >= 2 else [entry2]
                        if t_ag_pj is not None and wave == 0 and ob == 0 \
                                and hb < 2:
                            dep = dep + [t_ag_pj]
                        ldk[hb] = G("sync" if par == 0 else "gpsimd", rwl,
                                    n_dma=4, chain=False, extra_deps=dep)
                        def mm2(e, c, hb=hb, par=par):
                            for j in range(4):
                                hh = hb * 4 + j
                                for tc in range(4):
                                    e.matmul(psAD[tc][:, :],
                                             A1(hh * 512, 512)[:, tc * 128:
                                                               (tc + 1) * 128],
                                             AR2[:, par * 4096 + j * 512:
                                                 par * 4096 + (j + 1) * 512],
                                             start=(hh == 0), stop=(hh == 39))
                        mmk[hb] = G("tensor", mm2, chain=(hb == 9),
                                    extra_deps=[ldk[hb]] +
                                    ([mmk[hb - 1]] if hb else []))
                    def ra(e, c, wave=wave, ob=ob):
                        for tc in range(4):
                            i = wave * 4 + tc
                            e.scalar_tensor_tensor(
                                xs[i][:, ob * 512:(ob + 1) * 512],
                                psAD[tc][:, :], small[:, 15:16],
                                xs[i][:, ob * 512:(ob + 1) * 512],
                                OP.mult, OP.add)
                    G("vector", ra)

        def attn_b():
            w_entry = last_tok[0]
            t_ag = ag(ln_mine, ln_all)
            pj = new_proj_state()
            t_tb = {}
            for tb in range(4):
                t_ldt = ldT_g(ln_all, tb,
                              [t_ag] if tb == 0 else [t_tb[tb - 1]])
                for gg in range(GPC):
                    proj_job(pj, t_ldt, wk_b, gg * 128, None, None, None, tb,
                             direct_dst=A1(4096 + gg * 2048, 2048)
                             [:, tb * 512:(tb + 1) * 512], w_entry=w_entry)
                for h in range(HPC):
                    proj_job(pj, t_ldt, wq_b, h * 128, None, qt_sp, h, tb,
                             w_entry=w_entry)
                t_tb[tb] = v_job(
                    pj, t_ldt, wv_b, 0, tb,
                    tc_dst=lambda tc, tb=tb: A1((tb * 4 + tc) * 256, 256),
                    w_entry=w_entry)
            G("vector", lambda e, c: None,
              extra_deps=[t_tb[3], pj["tail"][-1],
                          pj["last_sp"][0], pj["last_sp"][1]])
            for gg in range(GPC):
                G("sync", lambda e, c: ld_pat(e, c, cos_bk, sin_bk), n_dma=2)
                gla_prep(gg)
                rope("b", A1(4096 + gg * 2048, 2048),
                     A1(4096 + gg * 2048, 2048), inplace=True)
                G("sync", lambda e, c: ld_pat(e, c, cos_bq, sin_bq), n_dma=2)
                for hh in range(4):
                    h = gg * 4 + hh
                    G("sync", lambda e, c, h=h: c.dma(
                        e, A1(16384, 2048)[:, :],
                        qt_sp[h * 128:(h + 1) * 128, :]), n_dma=1)
                    gla_head(gg, h)
                    rope("b", A1(16384, 2048), A1(18432, 2048))
                    smax_head("b", h, A1(4096 + gg * 2048, 2048),
                              A1(22528, 2048))
                    G("vector", lambda e, c: e.tensor_tensor(
                        A1(22528, 2048)[:, :], A1(22528, 2048)[:, :],
                        A1(20480, 2048)[:, :], OP.add))
                    def yst(e, c, h=h):
                        c.dma(e, yt_mine[h * 128:(h + 1) * 128, :],
                              A1(22528, 2048)[:, 0:1024])
                        c.dma(e, yt_mine[1024 + h * 128:1024 + (h + 1) * 128,
                                         :], A1(22528, 2048)[:, 1024:2048])
                    G("gpsimd", yst, n_dma=2)
            ag(yt_mine, yt_all)
            wo_resid(yt_all_full, wo_b, False)

        def gla_prep(gg):
            ktb = A1(4096 + gg * 2048, 2048)
            G("scalar", lambda e, c: e.activation(AR3[:, 2048:4096],
                                                  ktb[:, :], AF.Sigmoid))
            G("scalar", lambda e, c: e.activation(AR3[:, 0:2048],
                                                  AR3[:, 2048:4096], AF.Ln))
            G("vector", lambda e, c: e.tensor_scalar(
                AR3[:, 0:2048], AR3[:, 0:2048], -1.0, None, OP.mult))
            cur, oth = 0, 2048
            sh = 1
            while sh < 2048:
                def pd(e, c, cur=cur, oth=oth, sh=sh):
                    e.tensor_tensor(AR3[:, oth + sh:oth + 2048],
                                    AR3[:, cur + sh:cur + 2048],
                                    AR3[:, cur:cur + 2048 - sh], OP.add)
                    e.tensor_copy(AR3[:, oth:oth + sh], AR3[:, cur:cur + sh])
                G("vector", pd)
                cur, oth = oth, cur
                sh *= 2
            assert cur == 2048, "pcum must land in slot1"
            pcum = AR3[:, 2048:4096]
            for ci in range(16):
                c0 = ci * 128
                G("tensor", lambda e, c, c0=c0, ktb=ktb: e.transpose(
                    psT[:, 0:128], ktb[:, c0:c0 + 128], idt[:, :]))
                G("vector", lambda e, c: e.reduce_max(
                    sc1[:, 0:1], psT[:, 0:128], AX.X, negate=True))
                G("scalar", lambda e, c: e.activation(
                    scr1[:, :], psT[:, 0:128], AF.Exp, bias=sc1[:, 0:1],
                    accum_out=sc1[:, 1:2]))
                def knr(e, c):
                    e.reciprocal(sc1[:, 2:3], sc1[:, 1:2])
                    e.drain()
                    e.tensor_scalar(scr2[:, :], scr1[:, :], sc1[:, 2:3],
                                    None, OP.mult)
                G("vector", knr)
                G("tensor", lambda e, c: e.transpose(psT[:, 128:256],
                                                     scr2[:, :], idt[:, :]))
                G("vector", lambda e, c, c0=c0: e.tensor_copy(
                    A1(10240, 2048)[:, c0:c0 + 128], psT[:, 128:256]))
            ksT = A1(10240, 2048)
            prevdone = None
            for ci in range(16):
                c0 = ci * 128
                def gf(e, c, ci=ci, c0=c0, pcum=pcum):
                    if ci > 0:
                        e.tensor_scalar(sc1[:, 0:1], pcum[:, c0 - 1:c0],
                                        1.0 / GN, None, OP.mult)
                        e.drain()
                        e.tensor_scalar(sc1[:, 1:2], sc1[:, 0:1], -1.0, None,
                                        OP.mult)
                    e.tensor_scalar(sc1[:, 2:3], pcum[:, c0 + 127:c0 + 128],
                                    -1.0 / GN, None, OP.mult)
                G("vector", gf)
                def ge(e, c, ci=ci, c0=c0, pcum=pcum):
                    if ci > 0:
                        e.activation(A1(8192, 2048)[:, c0:c0 + 128],
                                     pcum[:, c0:c0 + 128], AF.Exp,
                                     bias=sc1[:, 0:1], scale=-1.0 / GN)
                        e.activation(scr1[:, :], pcum[:, c0:c0 + 128], AF.Exp,
                                     bias=sc1[:, 1:2], scale=1.0 / GN)
                    else:
                        e.activation(A1(8192, 2048)[:, c0:c0 + 128],
                                     pcum[:, c0:c0 + 128], AF.Exp,
                                     scale=-1.0 / GN)
                        e.activation(scr1[:, :], pcum[:, c0:c0 + 128], AF.Exp,
                                     scale=1.0 / GN)
                    e.activation(scr2[:, :], pcum[:, c0:c0 + 128], AF.Exp,
                                 bias=sc1[:, 2:3], scale=1.0 / GN)
                    if ci > 0:
                        e.activation(sc1[:, 3:4], pcum[:, c0 + 127:c0 + 128],
                                     AF.Exp, bias=sc1[:, 0:1], scale=-1.0 / GN)
                G("scalar", ge)
                def ktl(e, c, c0=c0, ksT=ksT):
                    e.tensor_tensor(scr2[:, :], ksT[:, c0:c0 + 128],
                                    scr2[:, :], OP.mult)
                    e.drain()
                    e.tensor_tensor(ksT[:, c0:c0 + 128], ksT[:, c0:c0 + 128],
                                    scr1[:, :], OP.mult)
                G("vector", ktl)
                G("tensor", lambda e, c: e.transpose(psT[:, 256:384],
                                                     scr2[:, :], idt[:, :]))
                G("vector", lambda e, c, c0=c0: e.tensor_copy(
                    A1(12288, 2048)[:, c0:c0 + 128], psT[:, 256:384]))
                def smm(e, c, ci=ci, c0=c0, gg=gg):
                    e.matmul(psF[:, 0:128], A1(12288, 2048)[:, c0:c0 + 128],
                             A1(ci * 256, 256)[:, gg * 128:(gg + 1) * 128],
                             start=True, stop=True)
                G("tensor", smm)
                def supd(e, c, ci=ci, c0=c0):
                    if ci == 0:
                        e.tensor_copy(gsA[:, 0:128], psF[:, 0:128])
                    else:
                        e.tensor_scalar(gsA[:, ci * 128:(ci + 1) * 128],
                                        gsA[:, (ci - 1) * 128:ci * 128],
                                        sc1[:, 3:4], None, OP.mult)
                        e.drain()
                        e.tensor_tensor(gsA[:, ci * 128:(ci + 1) * 128],
                                        gsA[:, ci * 128:(ci + 1) * 128],
                                        psF[:, 0:128], OP.add)
                    e.drain()
                    e.tensor_copy(A1(14336, 2048)[:, c0:c0 + 128],
                                  gsA[:, ci * 128:(ci + 1) * 128])
                G("vector", supd)

        def gla_head(gg, h):
            qt = A1(16384, 2048)
            ktil = A1(10240, 2048)
            gstb = A1(14336, 2048)
            edec = A1(8192, 2048)
            ogla = A1(20480, 2048)
            for ci in range(16):
                c0 = ci * 128
                G("tensor", lambda e, c, c0=c0, qt=qt: e.transpose(
                    psT[:, 384:512], qt[:, c0:c0 + 128], idt[:, :]))
                G("vector", lambda e, c: e.reduce_max(
                    sc1[:, 8:9], psT[:, 384:512], AX.X, negate=True))
                G("scalar", lambda e, c: e.activation(
                    scr1[:, :], psT[:, 384:512], AF.Exp, bias=sc1[:, 8:9],
                    accum_out=sc1[:, 9:10]))
                def qnr(e, c):
                    e.reciprocal(sc1[:, 10:11], sc1[:, 9:10])
                    e.drain()
                    e.tensor_scalar(sc1[:, 10:11], sc1[:, 10:11], 0.5, None,
                                    OP.mult)
                    e.drain()
                    e.tensor_scalar(scr2[:, :], scr1[:, :], sc1[:, 10:11],
                                    None, OP.mult)
                G("vector", qnr)
                G("tensor", lambda e, c: e.transpose(psT[:, 512:640],
                                                     scr2[:, :], idt[:, :]))
                G("vector", lambda e, c, c0=c0, edec=edec: e.tensor_tensor(
                    scr1[:, :], psT[:, 512:640], edec[:, c0:c0 + 128],
                    OP.mult))
                G("tensor", lambda e, c, c0=c0, ktil=ktil: e.matmul(
                    psE[:, 0:128], ktil[:, c0:c0 + 128], scr1[:, :],
                    start=True, stop=True))
                G("vector", lambda e, c: e.tensor_tensor(
                    scr2[:, :], psE[:, 0:128], trim[:, :], OP.mult))
                def omm(e, c, ci=ci, c0=c0, gg=gg, gstb=gstb):
                    e.matmul(psF[:, 0:128],
                             A1(ci * 256, 256)[:, gg * 128:(gg + 1) * 128],
                             scr2[:, :], start=True, stop=(ci == 0))
                    if ci > 0:
                        e.matmul(psF[:, 0:128],
                                 gstb[:, (ci - 1) * 128:ci * 128],
                                 scr1[:, :], start=False, stop=True)
                G("tensor", omm)
                G("vector", lambda e, c, c0=c0, ogla=ogla: e.tensor_copy(
                    ogla[:, c0:c0 + 128], psF[:, 0:128]))

        t_sq = ln_phase()
        tern_toks = tern_phase(t_sq, t_bbc)
        attn_a(tern_toks)
        if N_SUBLAYERS >= 2:
            ln_phase()
            ffn(tern_toks)
        if N_SUBLAYERS >= 3:
            ln_phase()
            attn_b()
        if N_SUBLAYERS >= 4:
            ln_phase()
            ffn(tern_toks)

        def store_out(e, c):
            for i in range(NTT):
                c.dma(e, out_t[i * 128:(i + 1) * 128, :], xs[i][:, :])
        G("gpsimd", store_out, n_dma=NTT)
        G("gpsimd", lambda e, c: None)

        WAIT = [None] * len(groups)
        cnt = {k: 0 for k in sems}
        OWNER = [None] * len(groups)
        for gi, (eng, kind, n_dma, deps, emit, dsem) in enumerate(groups):
            if kind == "dma":
                key = dsem or eng
                cnt[key] += 16 * n_dma
                WAIT[gi] = (key, cnt[key])
            elif kind == "cc":
                cnt["cc"] += 1
                WAIT[gi] = ("cc", cnt["cc"])
            elif eng in ("sync", "gpsimd"):
                WAIT[gi] = (eng, cnt[eng])
            else:
                cnt[eng] += 1
                WAIT[gi] = (eng, cnt[eng])
            OWNER[gi] = eng

        byeng = {e: [gi for gi, g in enumerate(groups) if g[0] == e]
                 for e in ENGS}
        pc = {e: 0 for e in ENGS}
        done = [False] * len(groups)
        progress = True
        while progress:
            progress = False
            for e in ENGS:
                while pc[e] < len(byeng[e]):
                    gi = byeng[e][pc[e]]
                    if all(done[d] for d in groups[gi][3]):
                        done[gi] = True
                        pc[e] += 1
                        progress = True
                    else:
                        break
        stuck = [e for e in ENGS if pc[e] < len(byeng[e])]
        if stuck:
            msg = []
            for e in stuck:
                gi = byeng[e][pc[e]]
                blockers = [d for d in groups[gi][3] if not done[d]]
                msg.append(f"{e} stuck at group {gi} waiting {blockers} "
                           f"(owners {[groups[d][0] for d in blockers]})")
            raise RuntimeError("deadlock in group graph: " + "; ".join(msg))

        class EngProxy:
            def __init__(self, e):
                object.__setattr__(self, "_e", e)
                object.__setattr__(self, "last", None)
            def __getattr__(self, name):
                f = getattr(self._e, name)
                if callable(f):
                    def wrap(*a, **k):
                        r = f(*a, **k)
                        if hasattr(r, "then_inc"):
                            object.__setattr__(self, "last", r)
                        return r
                    return wrap
                return f

        class Ctx:
            def __init__(self, sem):
                self.sem = sem
                self.n = 0
            def dma(self, e, out, in_):
                e.dma_start(out=out, in_=in_).then_inc(self.sem, 16)
                self.n += 1
            def dma_t(self, e, out, in_):
                e.dma_start_transpose(out, in_).then_inc(self.sem, 16)
                self.n += 1
            def dma_nc(self, e, out, in_):
                e.dma_start(out=out, in_=in_,
                            allow_slow_non_contiguous=True
                            ).then_inc(self.sem, 16)
                self.n += 1

        with nc.Block() as block:
            def mk(eng):
                def body(e):
                    c = Ctx(sems[eng])
                    for gi, (geng, kind, n_dma, deps, emit,
                             dsem) in enumerate(groups):
                        if geng != eng:
                            continue
                        for d in deps:
                            deng, dval = WAIT[d]
                            e.wait_ge(sems[deng], dval)
                        c.sem = sems[dsem] if dsem else sems[eng]
                        n0 = c.n
                        px = EngProxy(e)
                        emit(px, c)
                        if kind == "dma":
                            assert c.n - n0 == n_dma, \
                                f"group {gi} declared {n_dma} dmas, did {c.n - n0}"
                        elif kind == "cc":
                            pass
                        elif eng in ("sync", "gpsimd"):
                            assert c.n == n0, f"group {gi} unexpected dma"
                        else:
                            assert c.n == n0, f"group {gi} unexpected dma"
                            if px.last is not None:
                                px.last.then_inc(sems[eng], 1)
                            else:
                                e.nop().then_inc(sems[eng], 1)
                return body
            for eng in ENGS:
                getattr(block, eng)(mk(eng))

    return nc


_NC_CACHE = [None]


def prepare_in_maps(inputs):
    consts = _np_consts()
    xin = np.asarray(inputs["L0_attn_a_hidden_states"], dtype=np.float32)
    w_qkv = np.asarray(inputs["L0_attn_a_w_qkv"], np.float32)
    b_qkv = np.asarray(inputs["L0_attn_a_b_qkv"], np.float32)
    w_out = np.ascontiguousarray(np.asarray(inputs["L0_attn_a_w_out"],
                                            np.float32))
    b_out = np.asarray(inputs["L0_attn_a_b_out"], np.float32)
    Wq = np.asarray(inputs["L0_attn_b_Wq"], np.float32)
    Wk = np.asarray(inputs["L0_attn_b_Wk"], np.float32)
    Wv = np.asarray(inputs["L0_attn_b_Wv"], np.float32)
    Wo = np.ascontiguousarray(np.asarray(inputs["L0_attn_b_Wo"], np.float32))
    w_fc = np.asarray(inputs["L0_ffn_w_fc"], np.float32)
    w_proj = np.asarray(inputs["L0_ffn_w_proj"], np.float32)

    qkv3 = w_qkv.reshape(HID, NH, 3 * D)
    b3 = b_qkv.reshape(NH, 3 * D)
    wfcT = np.ascontiguousarray(w_fc.T)
    wprojT = np.ascontiguousarray(w_proj.T)

    in_maps = []
    for c in range(8):
        b, p = c // 2, c % 2
        heads = list(range(HPC * p, HPC * (p + 1)))
        wqk_c = np.concatenate(
            [qkv3[:, g, 0:2 * D] for g in heads], axis=1)
        wva_c = np.concatenate(
            [qkv3[:, g, 2 * D:3 * D] for g in heads], axis=1)
        bq_c = np.concatenate([b3[g, 0:D] for g in heads])[:, None]
        bk_c = np.concatenate([b3[g, D:2 * D] for g in heads])[:, None]
        bv_c = np.concatenate([b3[g, 2 * D:3 * D] for g in heads])[:, None]
        m = {
            "x0": np.ascontiguousarray(xin[b, p * TOK:(p + 1) * TOK, :]),
            "wqk": np.ascontiguousarray(wqk_c),
            "wv_a": np.ascontiguousarray(wva_c),
            "b_q": np.ascontiguousarray(bq_c),
            "b_k": np.ascontiguousarray(bk_c),
            "b_v": np.ascontiguousarray(bv_c),
            "w_out": w_out,
            "b_out": np.ascontiguousarray(b_out[:, None]),
            "wq_b": np.ascontiguousarray(Wq[:, HPC * D * p: HPC * D * (p + 1)]),
            "wk_b": np.ascontiguousarray(Wk[:, GPC * D * p: GPC * D * (p + 1)]),
            "wv_b": np.ascontiguousarray(Wv[:, GPC * D * p: GPC * D * (p + 1)]),
            "wo_b": Wo,
            "wfc_sh": np.ascontiguousarray(wfcT[256 * c:256 * (c + 1), :]),
            "wpj_sh": np.ascontiguousarray(wprojT[640 * c:640 * (c + 1), :]),
        }
        m.update({k: consts[k] for k in
                  ("cos_a", "sin_a", "cos_bq", "sin_bq", "cos_bk", "sin_bk",
                   "perm16", "perm64", "ident", "cmask", "trimask", "constv")})
        in_maps.append(m)
    return in_maps


def kernel(**inputs):
    in_maps = prepare_in_maps(inputs)
    if _NC_CACHE[0] is None:
        _NC_CACHE[0] = build_nc()
    nc = _NC_CACHE[0]
    res = run_bass_kernel_spmd(nc, in_maps, core_ids=list(range(8)))
    out = np.zeros((B, S, HID), dtype=np.float32)
    for c in range(8):
        b, p = c // 2, c % 2
        out[b, p * TOK:(p + 1) * TOK, :] = res.results[c]["out"]
    return out


# version 18
# speedup vs baseline: 1.0478x; 1.0478x over previous
import contextlib
import math
import numpy as np
import ml_dtypes
import concourse.bass as bass
import concourse.mybir as mybir
from concourse.bass_utils import run_bass_kernel_spmd

F32 = mybir.dt.float32
BF16 = mybir.dt.bfloat16
AF = mybir.ActivationFunctionType
OP = mybir.AluOpType
AX = mybir.AxisListType

B, S, HID = 4, 2048, 2048
NH, D = 16, 128
HPC = 8
TOK = 1024
NTT = TOK // 128
GPC = 2
FFH = 5120
GN = 16.0
EPS = 1e-5
NEG = -1.0e30
N_SUBLAYERS = 4

CONST_VALS = [1e-5, -1.0, 1.0 / 16.0, -1.0 / 16.0, 0.5, 0.7,
              1.0 / (5120.0 * 2048.0), 0.0, 1.0, 2.0]

PAIRS = [[0, 1], [2, 3], [4, 5], [6, 7]]
ALL8 = [list(range(8))]
ENGS = ("tensor", "vector", "scalar", "sync", "gpsimd")


def _np_consts():
    c = {}
    t = np.arange(S, dtype=np.float64)
    inv_a = 1.0 / (10000.0 ** (np.arange(0, 32, 2, dtype=np.float64) / 32.0))
    fa = np.outer(t, inv_a)
    cos_a = np.zeros((32, S)); sin_a = np.zeros((32, S))
    for r in range(16):
        cos_a[r] = np.cos(fa[:, r]); cos_a[r + 16] = np.cos(fa[:, r])
        sin_a[r] = -np.sin(fa[:, r]); sin_a[r + 16] = np.sin(fa[:, r])
    c["cos_a"] = cos_a.astype(np.float32)
    c["sin_a"] = sin_a.astype(np.float32)

    inv_b = 1.0 / (10000.0 ** (np.arange(0, D, 2, dtype=np.float64) / D))
    fb = np.outer(t, inv_b)
    cos_b = np.zeros((D, S)); sin_b = np.zeros((D, S))
    for r in range(64):
        cos_b[r] = np.cos(fb[:, r]); cos_b[r + 64] = np.cos(fb[:, r])
        sin_b[r] = -np.sin(fb[:, r]); sin_b[r + 64] = np.sin(fb[:, r])
    sc = 1.0 / math.sqrt(D)
    c["cos_bq"] = (cos_b * sc).astype(np.float32)
    c["sin_bq"] = (sin_b * sc).astype(np.float32)
    c["cos_bk"] = cos_b.astype(np.float32)
    c["sin_bk"] = sin_b.astype(np.float32)

    p16 = np.zeros((128, 32), dtype=np.float32)
    for r in range(32):
        p16[r ^ 16, r] = 1.0
    c["perm16"] = p16.astype(ml_dtypes.bfloat16)
    p64 = np.zeros((128, 128), dtype=np.float32)
    for r in range(128):
        p64[r ^ 64, r] = 1.0
    c["perm64"] = p64.astype(ml_dtypes.bfloat16)
    c["ident"] = np.eye(128, dtype=np.float32).astype(ml_dtypes.bfloat16)

    m = np.zeros((128, 4 * 512), dtype=np.float32)
    for off in range(4):
        for i in range(128):
            m[i, off * 512 + off * 128 + i + 1: (off + 1) * 512] = NEG
    c["cmask"] = m.astype(ml_dtypes.bfloat16)
    c["trimask"] = np.triu(np.ones((128, 128), dtype=np.float32))
    cv = np.zeros((128, 16), dtype=np.float32)
    for i, v in enumerate(CONST_VALS):
        cv[:, i] = np.float32(v)
    c["constv"] = cv
    return c


def build_nc():
    nc = bass.Bass("TRN2", target_bir_lowering=False, debug=False,
                   detect_race_conditions=False)

    di = {}

    def dram_in(name, shape, dtype=F32):
        di[name] = nc.dram_tensor(name, list(shape), dtype,
                                  kind="ExternalInput").ap()
        return di[name]

    x0 = dram_in("x0", (TOK, HID))
    wqk = dram_in("wqk", (HID, HPC * 256))
    wv_a = dram_in("wv_a", (HID, HPC * 128))
    b_q = dram_in("b_q", (HPC * 128, 1))
    b_k = dram_in("b_k", (HPC * 128, 1))
    b_v = dram_in("b_v", (HPC * 128, 1))
    w_out = dram_in("w_out", (HID, HID))
    b_out = dram_in("b_out", (HID, 1))
    wq_b = dram_in("wq_b", (HID, HPC * 128))
    wk_b = dram_in("wk_b", (HID, GPC * 128))
    wv_b = dram_in("wv_b", (HID, GPC * 128))
    wo_b = dram_in("wo_b", (HID, HID))
    wfc_sh = dram_in("wfc_sh", (HID // 8, FFH))
    wpj_sh = dram_in("wpj_sh", (FFH // 8, HID))
    cos_a = dram_in("cos_a", (32, S))
    sin_a = dram_in("sin_a", (32, S))
    cos_bq = dram_in("cos_bq", (D, S))
    sin_bq = dram_in("sin_bq", (D, S))
    cos_bk = dram_in("cos_bk", (D, S))
    sin_bk = dram_in("sin_bk", (D, S))
    perm16_i = dram_in("perm16", (128, 32), BF16)
    perm64_i = dram_in("perm64", (128, 128), BF16)
    ident_i = dram_in("ident", (128, 128), BF16)
    cmask_i = dram_in("cmask", (128, 4 * 512), BF16)
    trimask_i = dram_in("trimask", (128, 128))
    constv_i = dram_in("constv", (128, 16))

    out_t = nc.dram_tensor("out", [TOK, HID], F32, kind="ExternalOutput").ap()

    ln_mine = nc.dram_tensor("ln_mine", [TOK, HID], BF16).ap()
    ln_all = nc.dram_tensor("ln_all", [2 * TOK, HID], BF16).ap()
    yt_mine = nc.dram_tensor("yt_mine", [2 * TOK, TOK], BF16).ap()
    yt_all_t = nc.dram_tensor("yt_all", [4 * TOK + 128, TOK], BF16)
    yt_all = yt_all_t.ap()[0:4 * TOK, :]
    yt_all_full = yt_all_t.ap()
    qt_sp = nc.dram_tensor("qt_sp", [HPC * 128, S], BF16).ap()
    kt_sp = nc.dram_tensor("kt_sp", [HPC * 128, S], BF16).ap()
    absr_m = nc.dram_tensor("absr_m", [1, 16], F32).ap()
    absr_s = nc.dram_tensor("absr_s", [1, 16], F32).ap()
    wfcq_m = nc.dram_tensor("wfcq_m", [HID // 8, FFH], BF16).ap()
    wfcq_all = nc.dram_tensor("wfcq_all", [HID, FFH], BF16).ap()
    wpjq_m = nc.dram_tensor("wpjq_m", [FFH // 8, HID], BF16).ap()
    wpjq_all = nc.dram_tensor("wpjq_all", [FFH, HID], BF16).ap()
    yt_stage = nc.dram_tensor("yt_stage", [2 * TOK, TOK], BF16).ap()

    groups = []
    last_tok = [None]

    def G(eng, emit, n_dma=0, kind=None, extra_deps=(), chain=True,
          dma_sem=None):
        k = kind or ("dma" if eng in ("sync", "gpsimd") and n_dma > 0 else "eng")
        deps = []
        if chain and last_tok[0] is not None:
            deps.append(last_tok[0])
        deps.extend(extra_deps)
        gi = len(groups)
        groups.append((eng, k, n_dma, deps, emit, dma_sem))
        if chain:
            last_tok[0] = gi
        return gi

    with contextlib.ExitStack() as ES:
        def sb(name, shape, dtype):
            return ES.enter_context(nc.sbuf_tensor(name, list(shape), dtype))

        def psum(name, shape, dtype):
            return ES.enter_context(nc.psum_tensor(name, list(shape), dtype))

        xs = [sb(f"xst{i}", (128, HID), F32) for i in range(NTT)]
        lnb = sb("lnb", (128, HID), BF16)
        AR1 = sb("AR1", (128, 24576), BF16)
        AR2 = sb("AR2", (128, 8192), BF16)
        AR3 = sb("AR3", (128, 6144), F32)
        gsA = sb("gsA", (128, 2048), F32)
        pat = sb("pat", (128, 4096), F32)
        maskT = sb("maskT", (128, 4 * 512), BF16)
        trim = sb("trim", (128, 128), F32)
        wta = sb("wta", (128, 2048), BF16)
        ptT = sb("ptT", (128, 2048), BF16)
        idt = sb("idt", (128, 128), BF16)
        prm16 = sb("prm16", (128, 32), BF16)
        prm64 = sb("prm64", (128, 128), BF16)
        scrq = sb("scrq", (128, 512), BF16)
        scrv = sb("scrv", (128, 512), BF16)
        scr1 = sb("scr1", (128, 128), BF16)
        scr2 = sb("scr2", (128, 128), BF16)
        stat = sb("stat", (128, 48), F32)
        aggr = sb("aggr", (128, 2), F32)
        sc1 = sb("sc1", (128, 16), F32)
        small = sb("small", (128, 16), F32)
        bqh = sb("bqh", (128, HPC), F32)
        bkh = sb("bkh", (128, HPC), F32)
        bvh = sb("bvh", (128, HPC), F32)
        brow = AR3[0:1, 4096:6144]
        onef = sc1[:, 11:12]
        cvals = sb("cvals", (128, 16), F32)
        for _ci, _cv in enumerate(CONST_VALS):
            nc.const_aps.register(F32, _cv, cvals[:, _ci:_ci + 1])

        psS_t = psum("psS", (128, 2048), F32)
        psS = psS_t.ap() if hasattr(psS_t, "ap") else psS_t
        psE = psum("psE", (128, 512), F32)
        psF = psum("psF", (128, 512), F32)
        psT = psum("psT", (128, 2048), BF16)
        psAD = [psS[:, i * 512:(i + 1) * 512] for i in range(4)]

        def A1(c0, w):
            return AR1[:, c0:c0 + w]

        sems = {e: ES.enter_context(nc.semaphore(f"s_{e}")) for e in ENGS}
        sems["cc"] = ES.enter_context(nc.semaphore("s_cc"))
        for dk in ("lnst0", "lnst1", "pwl0", "pwl1", "psp0", "psp1",
                   "pvl0", "pvl1", "ldt", "tld0", "tld1", "tql0", "tql1",
                   "tql2", "tsp0", "tsp1", "wwl0", "wwl1"):
            sems[dk] = ES.enter_context(nc.semaphore(f"s_{dk}"))


        def ld_consts(e, c):
            c.dma(e, pat[0:32, 0:2048], cos_a[:, :])
            c.dma(e, pat[0:32, 2048:4096], sin_a[:, :])
            c.dma(e, maskT[:, :], cmask_i[:, :])
            c.dma(e, trim[:, :], trimask_i[:, :])
            c.dma(e, idt[:, :], ident_i[:, :])
            c.dma(e, prm16[:, :], perm16_i[:, :])
            c.dma(e, prm64[:, :], perm64_i[:, :])
            c.dma_nc(e, bqh[:, :], b_q.rearrange("(h p) o -> p (h o)", p=128))
            c.dma_nc(e, bkh[:, :], b_k.rearrange("(h p) o -> p (h o)", p=128))
            c.dma_nc(e, bvh[:, :], b_v.rearrange("(h p) o -> p (h o)", p=128))
            c.dma(e, brow, b_out.rearrange("a o -> o a"))
            c.dma(e, cvals[:, :], constv_i[:, :])
        G("sync", ld_consts, n_dma=12)

        def ld_x(e, c):
            for i in range(NTT):
                c.dma(e, xs[i][:, :], x0[i * 128:(i + 1) * 128, :])
        G("sync", ld_x, n_dma=NTT)

        G("vector", lambda e, c: e.memset(AR3[0:1, 0:128], 1.0))
        for k in range(4):
            G("tensor", lambda e, c, k=k: e.matmul(
                psE[:, :], AR3[0:1, 0:128],
                brow[0:1, k * 512:(k + 1) * 512], start=True, stop=True))
            G("vector", lambda e, c, k=k: e.tensor_copy(
                gsA[:, k * 512:(k + 1) * 512], psE[:, :]))
        t_bbc = last_tok[0]

        lnbufs = [lnb, ptT]

        def ln_phase():
            def sums(e, c):
                for i in range(NTT):
                    e.reduce_sum(stat[:, i:i + 1], xs[i][:, :], AX.X)
            G("vector", sums)
            def sqs(e, c):
                for i in range(NTT):
                    e.activation(AR3[:, 0:2048], xs[i][:, :], AF.Square,
                                 accum_out=stat[:, 8 + i:9 + i])
            t_sq = G("scalar", sqs)
            def mv(e, c):
                e.tensor_scalar(stat[:, 16:24], stat[:, 0:8], 1.0 / HID,
                                None, OP.mult)
                e.tensor_scalar(stat[:, 8:16], stat[:, 8:16], 1.0 / HID,
                                None, OP.mult)
                e.drain()
                e.tensor_tensor(stat[:, 24:32], stat[:, 16:24],
                                stat[:, 16:24], OP.mult)
                e.drain()
                e.tensor_tensor(stat[:, 24:32], stat[:, 8:16],
                                stat[:, 24:32], OP.subtract)
            G("vector", mv)
            G("scalar", lambda e, c: e.activation(stat[:, 24:32],
                                                  stat[:, 24:32], AF.Sqrt,
                                                  bias=EPS))
            def inv(e, c):
                e.reciprocal(stat[:, 32:40], stat[:, 24:32])
                e.drain()
                e.tensor_tensor(stat[:, 40:48], stat[:, 16:24],
                                stat[:, 32:40], OP.mult)
                e.drain()
                e.tensor_scalar(stat[:, 40:48], stat[:, 40:48], -1.0, None,
                                OP.mult)
            t_inv = G("vector", inv)
            t_dma = {}
            for i in range(NTT):
                buf = lnbufs[i % 2]
                dep_n = [t_inv] if i < 2 else [t_dma[i - 2]]
                t_n = G("scalar", lambda e, c, i=i, buf=buf: e.activation(
                    buf[:, :], xs[i][:, :], AF.Identity,
                    bias=stat[:, 40 + i:41 + i], scale=stat[:, 32 + i:33 + i]),
                    chain=False, extra_deps=dep_n)
                t_dma[i] = G("gpsimd", lambda e, c, i=i, buf=buf: c.dma(
                    e, ln_mine[i * 128:(i + 1) * 128, :], buf[:, :]), n_dma=1,
                    chain=False, extra_deps=[t_n],
                    dma_sem=f"lnst{i % 2}")
            G("vector", lambda e, c: None,
              extra_deps=[t_dma[NTT - 2], t_dma[NTT - 1]])
            return t_sq

        def ag(bi, bo, grps=PAIRS, op="AllGather", chain=True, extra_deps=()):
            def fn(e, c):
                e.collective_compute(op, OP.add if op == "AllReduce"
                                     else OP.bypass, replica_groups=grps,
                                     ins=[bi.opt()], outs=[bo.opt()],
                                     ).then_inc(sems["cc"], 1)
            return G("gpsimd", fn, kind="cc", chain=chain,
                     extra_deps=extra_deps)

        def ldT(src, tb, n_tok=512):
            def fn(e, c):
                for f in range(16):
                    c.dma_t(e, AR2[:, f * 512:f * 512 + n_tok],
                            src[tb * 512:tb * 512 + n_tok,
                                f * 128:(f + 1) * 128])
            G("sync", fn, n_dma=16, dma_sem="ldt")

        def lnT(f):
            return AR2[:, f * 512:(f + 1) * 512]

        def tern_phase(t_sq, t_bbc):
            tc_last = [None]

            def T(eng, emit, n_dma=0, kind=None, extra=()):
                deps = list(extra)
                if tc_last[0] is not None:
                    deps.append(tc_last[0])
                tok = G(eng, emit, n_dma=n_dma, kind=kind, chain=False,
                        extra_deps=deps)
                tc_last[0] = tok
                return tok

            t_ms = G("vector", lambda e, c: (e.memset(small[:, :], 0.0),
                                             e.memset(onef, 1.0)),
                     chain=False, extra_deps=[t_bbc])
            stripes = []
            for r in range(2):
                for c0, w in ((0, 2048), (2048, 2048), (4096, 1024)):
                    stripes.append(("fc", r, c0, w))
            stripes += [("pj", r, 0, 2048) for r in range(5)]

            t_ld, t_ab = {}, {}
            for i, (kind, r, c0, w) in enumerate(stripes):
                src = wfc_sh if kind == "fc" else wpj_sh
                bz = (i % 2) * 2048
                dep = [t_ab[i - 2]] if i >= 2 else \
                    ([t_sq] if i % 2 == 0 else [t_bbc])
                t_ld[i] = G("sync", lambda e, c, src=src, r=r, c0=c0, w=w,
                            bz=bz: c.dma(
                                e, AR3[:, bz:bz + w],
                                src[r * 128:(r + 1) * 128, c0:c0 + w]),
                            n_dma=1, chain=False, extra_deps=dep,
                            dma_sem=f"tld{i % 2}")
                t_ab[i] = G("scalar", lambda e, c, w=w, bz=bz, i=i:
                            e.activation(AR3[:, 4096:4096 + w],
                                         AR3[:, bz:bz + w],
                                         AF.Abs, accum_out=small[:, i:i + 1]),
                            chain=False, extra_deps=[t_ld[i], t_ms])
            tc_last[0] = t_ab[10]
            T("tensor", lambda e, c: e.matmul(psE[0:1, 0:11], onef,
                                              small[:, 0:11], start=True,
                                              stop=True), extra=[t_bbc])
            toks = {"stripes": stripes, "t_ab": t_ab}

            def emit_mid():
                def colred(e, c):
                    e.tensor_copy(small[0:1, 0:11], psE[0:1, 0:11])
                    e.drain()
                    e.tensor_reduce(small[0:1, 12:13], small[0:1, 0:6], AX.X,
                                    OP.add)
                    e.tensor_reduce(small[0:1, 13:14], small[0:1, 6:11],
                                    AX.X, OP.add)
                T("vector", colred)
                T("gpsimd", lambda e, c: c.dma(e, absr_m[0:1, 0:16],
                                               small[0:1, 0:16]), n_dma=1)
                def arfn(e, c):
                    e.collective_compute("AllReduce", OP.add,
                                         replica_groups=ALL8,
                                         ins=[absr_m.opt()],
                                         outs=[absr_s.opt()],
                                         ).then_inc(sems["cc"], 1)
                T("gpsimd", arfn, kind="cc")
                T("gpsimd", lambda e, c: c.dma(e, small[0:1, 6:8],
                                               absr_s[0:1, 12:14]), n_dma=1)
                def amth(e, c):
                    e.tensor_scalar(small[0:1, 8:9], small[0:1, 6:7],
                                    1.0 / (FFH * HID), 1e-5, OP.mult, OP.max)
                    e.tensor_scalar(small[0:1, 9:10], small[0:1, 7:8],
                                    1.0 / (FFH * HID), 1e-5, OP.mult, OP.max)
                    e.drain()
                    e.tensor_scalar(small[0:1, 10:11], small[0:1, 8:9], 0.7,
                                    None, OP.mult)
                    e.tensor_scalar(small[0:1, 11:12], small[0:1, 9:10], 0.7,
                                    None, OP.mult)
                    e.tensor_scalar(small[0:1, 12:13], small[0:1, 8:9],
                                    small[0:1, 8:9], None, OP.mult)
                    e.drain()
                    e.tensor_scalar(small[0:1, 12:13], small[0:1, 12:13],
                                    small[0:1, 9:10], None, OP.mult)
                T("vector", amth)
                T("vector", lambda e, c: e.memset(AR3[0:1, 128:256], 1.0))
                T("tensor", lambda e, c: e.matmul(
                    psE[:, 0:3], AR3[0:1, 128:256], small[0:1, 10:13],
                    start=True, stop=True))
                toks["ps_free"] = T("vector", lambda e, c: e.tensor_copy(
                    small[:, 13:16], psE[:, 0:3]))
                def nthr(e, c):
                    e.tensor_scalar(sc1[:, 6:7], small[:, 13:14], -1.0, None,
                                    OP.mult)
                    e.tensor_scalar(sc1[:, 7:8], small[:, 14:15], -1.0, None,
                                    OP.mult)
                T("vector", nthr)
            toks["emit_mid"] = emit_mid

            def emit_quant(t_ar2_free):
                t_qld, t_tq, t_sp = {}, {}, {}
                for i, (kind, r, c0, w) in enumerate(stripes):
                    src = wfc_sh if kind == "fc" else wpj_sh
                    dst = wfcq_m if kind == "fc" else wpjq_m
                    thr = small[:, 13:14] if kind == "fc" \
                        else small[:, 14:15]
                    nth = sc1[:, 6:7] if kind == "fc" else sc1[:, 7:8]
                    bz = (i % 3) * 2048
                    sz = (i % 2) * 2048
                    dep = [t_tq[i - 3]] if i >= 3 else \
                        [t_ab[10], toks["ps_free"]]
                    t_qld[i] = G("gpsimd", lambda e, c, src=src, r=r, c0=c0,
                                 w=w, bz=bz: c.dma(
                                     e, AR3[:, bz:bz + w],
                                     src[r * 128:(r + 1) * 128, c0:c0 + w]),
                                 n_dma=1, chain=False, extra_deps=dep,
                                 dma_sem=f"tql{i % 3}")
                    dep_tq = [t_qld[i], tc_last[0]] + \
                        ([t_sp[i - 2]] if i >= 2 else [t_ar2_free])
                    def tq(e, c, w=w, thr=thr, nth=nth, bz=bz, sz=sz):
                        e.tensor_scalar(AR2[:, sz:sz + w], AR3[:, bz:bz + w],
                                        thr, None, OP.is_gt)
                        e.drain()
                        e.tensor_scalar(AR3[:, bz:bz + w], AR3[:, bz:bz + w],
                                        nth, None, OP.is_lt)
                        e.drain()
                        e.tensor_tensor(AR2[:, sz:sz + w], AR2[:, sz:sz + w],
                                        AR3[:, bz:bz + w], OP.subtract)
                    t_tq[i] = G("vector", tq, chain=False, extra_deps=dep_tq)
                    tc_last[0] = t_tq[i]
                    t_sp[i] = G("gpsimd", lambda e, c, dst=dst, r=r, c0=c0,
                                w=w, sz=sz: c.dma(
                                    e, dst[r * 128:(r + 1) * 128, c0:c0 + w],
                                    AR2[:, sz:sz + w]), n_dma=1, chain=False,
                                extra_deps=[t_tq[i]], dma_sem=f"tsp{i % 2}")
                toks["chain"] = t_sp[10]
                toks["sp9"] = t_sp[9]
                toks["ag_fc"] = ag(wfcq_m, wfcq_all, ALL8, chain=False,
                                   extra_deps=[t_sp[10], t_sp[9]])
                toks["ag_pj"] = ag(wpjq_m, wpjq_all, ALL8, chain=False,
                                   extra_deps=[toks["ag_fc"]])
            toks["emit_quant"] = emit_quant
            return toks

        def rope(kind, srcT, dstT, inplace=False):
            if kind == "a":
                prm, nr = prm16, 32
            else:
                prm, nr = prm64, 128
            for nt in range(4):
                sl = slice(nt * 512, (nt + 1) * 512)
                G("tensor", lambda e, c, sl=sl, prm=prm, nr=nr: e.matmul(
                    psF[0:nr, 0:512], prm[:, 0:nr], srcT[:, sl],
                    start=True, stop=True))
                def tts(e, c, sl=sl, nr=nr, inplace=inplace):
                    e.tensor_tensor(AR3[0:nr, 0:512], psF[0:nr, 0:512],
                                    pat[0:nr, 2048 + sl.start:2048 + sl.stop],
                                    OP.mult)
                    e.tensor_tensor(AR3[0:nr, 512:1024], srcT[0:nr, sl],
                                    pat[0:nr, sl], OP.mult)
                    if nr == 32 and not inplace:
                        e.tensor_copy(dstT[:, sl], srcT[:, sl])
                    e.drain()
                    e.tensor_tensor(dstT[0:nr, sl], AR3[0:nr, 0:512],
                                    AR3[0:nr, 512:1024], OP.add)
                G("vector", tts)

        def ld_pat(e, c, cosd, sind):
            c.dma(e, pat[0:cosd.shape[0], 0:2048], cosd[:, :])
            c.dma(e, pat[0:sind.shape[0], 2048:4096], sind[:, :])

        def smax_head(kind, h, ktbuf, ytile):
            entry = last_tok[0]
            toks = {}

            def pbw(qc, w):
                return lnb[:, 0:w] if qc % 2 == 0 else AR2[:, 0:w]

            def pb(qc, b_):
                return pbw(qc, 2048)[:, b_ * 128:(b_ + 1) * 128]

            def mk_smm(qc):
                ktmax = qc // 4 + 1
                def smm(e, c, qc=qc, ktmax=ktmax, ktbuf=ktbuf):
                    for kt in range(ktmax):
                        e.matmul(psAD[kt][:, :],
                                 A1(18432, 2048)[:, qc * 128:(qc + 1) * 128],
                                 ktbuf[:, kt * 512:(kt + 1) * 512],
                                 start=True, stop=True)
                toks[("smm", qc)] = G(
                    "tensor", smm, chain=False,
                    extra_deps=[toks[("ex", qc - 1)]] if qc else [entry])

            mk_smm(0)
            for qc in range(16):
                ktmax = qc // 4 + 1
                wid = 512 * ktmax
                def mx(e, c, qc=qc, ktmax=ktmax, wid=wid):
                    off = qc % 4
                    e.tensor_tensor(psAD[ktmax - 1][:, :],
                                    psAD[ktmax - 1][:, :],
                                    maskT[:, off * 512:(off + 1) * 512],
                                    OP.add)
                    if kind == "a":
                        e.drain()
                        e.reduce_max(sc1[:, 4:5], psS[:, 0:wid], AX.X,
                                     negate=True)
                t_mx = G("vector", mx, chain=False,
                         extra_deps=[toks[("smm", qc)]])
                ac = 4 + (qc % 2) * 2
                def ex(e, c, qc=qc, wid=wid, ac=ac):
                    if kind == "a":
                        e.activation(pbw(qc, wid), psS[:, 0:wid], AF.Exp,
                                     bias=sc1[:, 4:5],
                                     accum_out=small[:, ac:ac + 1])
                    else:
                        e.activation(pbw(qc, wid), psS[:, 0:wid], AF.Exp,
                                     accum_out=small[:, ac:ac + 1])
                t_ex = G("scalar", ex, chain=False,
                         extra_deps=[t_mx] + ([toks[("tp", qc - 2)]]
                                              if qc >= 2 else []))
                toks[("ex", qc)] = t_ex
                if qc + 1 < 16:
                    mk_smm(qc + 1)
                def nrm(e, c, qc=qc, wid=wid, ac=ac):
                    e.reciprocal(small[:, ac + 1:ac + 2], small[:, ac:ac + 1])
                    e.drain()
                    if kind == "b":
                        e.tensor_scalar(small[:, ac + 1:ac + 2],
                                        small[:, ac + 1:ac + 2], 0.5,
                                        None, OP.mult)
                        e.drain()
                    e.tensor_scalar(pbw(qc, wid), pbw(qc, wid),
                                    small[:, ac + 1:ac + 2], None, OP.mult)
                t_nrm = G("vector", nrm, chain=False, extra_deps=[t_ex])
                nblk = qc + 1
                def tp(e, c, qc=qc, nblk=nblk):
                    for b_ in range(nblk):
                        e.transpose(psT[:, b_ * 128:(b_ + 1) * 128],
                                    pb(qc, b_), idt[:, :])
                t_tp = G("tensor", tp, chain=False,
                         extra_deps=[t_nrm] + ([toks[("pe", qc - 1)]] if qc
                                               else []))
                toks[("tp", qc)] = t_tp
                t_pe = G("vector", lambda e, c, nblk=nblk: e.tensor_copy(
                    ptT[:, 0:nblk * 128], psT[:, 0:nblk * 128]),
                    chain=False, extra_deps=[t_tp])
                toks[("pe", qc)] = t_pe
                def pv(e, c, nblk=nblk, h=h):
                    for b_ in range(nblk):
                        if kind == "a":
                            lhs = A1(b_ * 1024, 1024)[:, h * 128:(h + 1) * 128]
                        else:
                            gg = h // 4
                            lhs = A1(b_ * 256, 256)[:, gg * 128:(gg + 1) * 128]
                        e.matmul(psE[:, 0:128], lhs,
                                 ptT[:, b_ * 128:(b_ + 1) * 128],
                                 start=(b_ == 0), stop=(b_ == nblk - 1))
                t_pv = G("tensor", pv, chain=False,
                         extra_deps=[t_pe] + ([toks[("ye", qc - 1)]] if qc
                                              else []))
                if kind == "a":
                    t_ye = G("vector", lambda e, c, qc=qc, h=h:
                             e.tensor_scalar(
                                 ytile[:, qc * 128:(qc + 1) * 128],
                                 psE[:, 0:128], bvh[:, h:h + 1], None,
                                 OP.add),
                             chain=(qc == 15), extra_deps=[t_pv])
                else:
                    t_ye = G("vector", lambda e, c, qc=qc: e.tensor_copy(
                        ytile[:, qc * 128:(qc + 1) * 128], psE[:, 0:128]),
                        chain=(qc == 15), extra_deps=[t_pv])
                toks[("ye", qc)] = t_ye
        def new_proj_state():
            return {"j": 0, "vt": 0,
                    "last_mm": {0: None, 1: None},
                    "last_ep": {0: None, 1: None},
                    "last_sp": {0: None, 1: None},
                    "last_vmm": {0: None, 1: None},
                    "last_vev": {0: None, 1: None},
                    "last_vld": None,
                    "tail": []}

        def proj_job(pj, tb_tok, wdram, col0, bias_ap, spill, h, tb,
                     direct_dst=None, w_entry=None, extra_mm=()):
            j = pj["j"]; pj["j"] += 1
            par = j % 2
            wbuf = wta if par == 0 else ptT
            sbuf = scrq if par == 0 else scrv
            ps = psE if par == 0 else psF
            dep_wl = ([pj["last_mm"][par]] if pj["last_mm"][par] is not None
                      else ([w_entry] if w_entry is not None else [tb_tok]))
            def wl(e, c, col0=col0, wbuf=wbuf, wdram=wdram):
                c.dma(e, wbuf.rearrange("p (f c) -> p f c", f=16),
                      wdram[:, col0:col0 + 128].rearrange(
                          "(f p) c -> p f c", p=128))
            t_wl = G("gpsimd", wl, n_dma=1,
                     chain=False, extra_deps=dep_wl, dma_sem=f"pwl{par}")
            dep_mm = [t_wl, tb_tok] + list(extra_mm) + (
                [pj["last_ep"][par]] if pj["last_ep"][par] is not None else [])
            def mm(e, c, wbuf=wbuf, ps=ps):
                for f in range(16):
                    e.matmul(ps[:, :], wbuf[:, f * 128:(f + 1) * 128], lnT(f),
                             start=(f == 0), stop=(f == 15))
            t_mm = G("tensor", mm, chain=False, extra_deps=dep_mm)
            pj["last_mm"][par] = t_mm
            if direct_dst is not None:
                t_ep = G("scalar", lambda e, c, ps=ps, d=direct_dst:
                         e.activation(d, ps[:, :], AF.Copy),
                         chain=False, extra_deps=[t_mm])
                pj["last_ep"][par] = t_ep
                pj["tail"].append(t_ep)
            else:
                dep_ep = [t_mm] + (
                    [pj["last_sp"][par]] if pj["last_sp"][par] is not None
                    else [])
                if bias_ap is None:
                    t_ep = G("scalar", lambda e, c, ps=ps, sbuf=sbuf:
                             e.activation(sbuf[:, :], ps[:, :], AF.Copy),
                             chain=False, extra_deps=dep_ep)
                else:
                    t_ep = G("scalar", lambda e, c, ps=ps, sbuf=sbuf,
                             bias_ap=bias_ap: e.activation(
                                 sbuf[:, :], ps[:, :], AF.Identity,
                                 bias=bias_ap), chain=False,
                             extra_deps=dep_ep)
                pj["last_ep"][par] = t_ep
                t_sp = G("gpsimd", lambda e, c, spill=spill, h=h, tb=tb,
                         sbuf=sbuf: c.dma(
                             e, spill[h * 128:(h + 1) * 128,
                                      tb * 512:(tb + 1) * 512], sbuf[:, :]),
                         n_dma=1, chain=False, extra_deps=[t_ep],
                         dma_sem=f"psp{par}")
                pj["last_sp"][par] = t_sp
                pj["tail"].append(t_sp)

        def v_job(pj, tb_tok, wdram, vcol0, tb, tc_dst, w_entry=None):
            vt = pj["vt"]; pj["vt"] += 1
            par = vt % 2
            vbuf = A1(16384 + par * 4096, 4096)
            dep_ld = ([pj["last_vmm"][par]] if pj["last_vmm"][par] is not None
                      else ([w_entry] if w_entry is not None else [tb_tok]))
            def vld(e, c, vbuf=vbuf, vcol0=vcol0, wdram=wdram):
                c.dma(e, vbuf.rearrange("p (f c) -> p f c", f=16),
                      wdram[:, vcol0:vcol0 + 256].rearrange(
                          "(f p) c -> p f c", p=128))
            t_ld = G("gpsimd", vld, n_dma=1,
                     chain=False, extra_deps=dep_ld, dma_sem=f"pvl{par}")
            pj["last_vld"] = t_ld
            prev_ev = pj["last_vev"][0] if (vt % 2 == 1) else \
                pj["last_vev"][1]
            dep_mm = [t_ld, tb_tok] + ([prev_ev] if prev_ev is not None
                                       else [])
            def vmm(e, c, vbuf=vbuf):
                for f in range(16):
                    for tc in range(4):
                        e.matmul(psAD[tc][:, 0:256],
                                 lnT(f)[:, tc * 128:(tc + 1) * 128],
                                 vbuf[:, f * 256:(f + 1) * 256],
                                 start=(f == 0), stop=(f == 15))
            t_mm = G("tensor", vmm, chain=False, extra_deps=dep_mm)
            pj["last_vmm"][par] = t_mm
            def vev(e, c, tc_dst=tc_dst):
                for tc in range(4):
                    e.activation(tc_dst(tc), psAD[tc][:, 0:256], AF.Copy)
            t_ev = G("scalar", vev, chain=False, extra_deps=[t_mm])
            pj["last_vev"][par] = t_ev
            pj["tail"].append(t_ev)
            return t_mm

        def ldT_g(src, tb, deps):
            def fn(e, c):
                for f in range(16):
                    c.dma_t(e, AR2[:, f * 512:f * 512 + 512],
                            src[tb * 512:(tb + 1) * 512,
                                f * 128:(f + 1) * 128])
            return G("sync", fn, n_dma=16, chain=False, extra_deps=deps,
                     dma_sem="ldt")

        def attn_a(tern_toks):
            w_entry = last_tok[0]
            t_ag = ag(ln_mine, ln_all)
            tern_toks["emit_mid"]()
            pj = new_proj_state()
            t_tb = {}
            for tb in range(4):
                t_ldt = ldT_g(ln_all, tb,
                              [t_ag] if tb == 0 else [t_tb[tb - 1]])
                for h in range(HPC):
                    em = [tern_toks["ps_free"]] if (tb == 0 and h == 0) \
                        else ()
                    proj_job(pj, t_ldt, wqk, h * 256, bqh[:, h:h + 1],
                             qt_sp, h, tb, w_entry=w_entry, extra_mm=em)
                    proj_job(pj, t_ldt, wqk, h * 256 + 128, bkh[:, h:h + 1],
                             kt_sp, h, tb, w_entry=w_entry, extra_mm=em)
                last_vmm = None
                for v4 in range(4):
                    last_vmm = v_job(
                        pj, t_ldt, wv_a, v4 * 256, tb,
                        tc_dst=lambda tc, tb=tb, v4=v4:
                            A1((tb * 4 + tc) * 1024, 1024)[:, v4 * 256:
                                                           (v4 + 1) * 256],
                        w_entry=w_entry)
                t_tb[tb] = last_vmm
            tern_toks["emit_quant"](t_tb[3])
            G("vector", lambda e, c: None,
              extra_deps=[t_tb[3], pj["tail"][-1],
                          pj["last_sp"][0], pj["last_sp"][1],
                          tern_toks["chain"], tern_toks["sp9"]])
            for h in range(HPC):
                G("sync", lambda e, c, h=h: c.dma(
                    e, A1(16384, 2048)[:, :], qt_sp[h * 128:(h + 1) * 128, :]),
                  n_dma=1)
                G("sync", lambda e, c, h=h: c.dma(
                    e, A1(20480, 2048)[:, :], kt_sp[h * 128:(h + 1) * 128, :]),
                  n_dma=1)
                rope("a", A1(16384, 2048), A1(18432, 2048))
                rope("a", A1(20480, 2048), A1(20480, 2048), inplace=True)
                smax_head("a", h, A1(20480, 2048), A1(22528, 2048))
                def yst(e, c, h=h):
                    c.dma(e, yt_mine[h * 128:(h + 1) * 128, :],
                          A1(22528, 2048)[:, 0:1024])
                    c.dma(e, yt_mine[1024 + h * 128:1024 + (h + 1) * 128, :],
                          A1(22528, 2048)[:, 1024:2048])
                G("gpsimd", yst, n_dma=2)
            ag(yt_mine, yt_all)
            wo_resid(yt_all_full, w_out, True)
            def badd(e, c):
                for i in range(NTT):
                    e.tensor_tensor(xs[i][:, :], xs[i][:, :], gsA[:, :],
                                    OP.add)
            G("vector", badd)

        _wo_ctr = [0]

        def wo_resid(ytall, wdram, has_wait_pid):
            _wo_ctr[0] += 1
            _rname = f"roff{_wo_ctr[0]}"
            def ystage(e, c):
                pid = nc.partition_id()
                with e.register(_rname) as r:
                    e.reg_mov(r, (pid % 2) * TOK)
                    roff = e.snap(r)
                    for sl in range(2):
                        c.dma(e, yt_stage[sl * TOK:(sl + 1) * TOK, :],
                              ytall[sl * 2048:sl * 2048 + 2 * TOK + 128]
                              [bass.ds(roff, TOK), :])
            G("sync", ystage, n_dma=2)
            def yld(e, c):
                for f in range(16):
                    r0 = (f // 8) * 1024 + (f % 8) * 128
                    c.dma(e, A1(f * 1024, 1024), yt_stage[r0:r0 + 128, :])
            G("sync", yld, n_dma=16)
            entry = last_tok[0]
            wbufs = [AR2, A1(16384, 8192)]
            t_wl, t_mm, t_ra = {}, {}, {}
            for k in range(8):
                thalf, ob = k // 4, k % 4
                buf = wbufs[k % 2]
                dep_wl = [t_mm[k - 2]] if k >= 2 else [entry]
                def wl(e, c, ob=ob, buf=buf):
                    c.dma(e, buf.rearrange("p (f c) -> p f c", f=16),
                          wdram[:, ob * 512:(ob + 1) * 512].rearrange(
                              "(f p) c -> p f c", p=128))
                t_wl[k] = G("gpsimd", wl, n_dma=1,
                            chain=False, extra_deps=dep_wl,
                            dma_sem=f"wwl{k % 2}")
                def mm(e, c, thalf=thalf, buf=buf):
                    for f in range(16):
                        for tc in range(4):
                            e.matmul(psAD[tc][:, :],
                                     A1(f * 1024, 1024)[:, thalf * 512 +
                                                        tc * 128:
                                                        thalf * 512 +
                                                        (tc + 1) * 128],
                                     buf[:, f * 512:(f + 1) * 512],
                                     start=(f == 0), stop=(f == 15))
                t_mm[k] = G("tensor", mm, chain=False,
                            extra_deps=[t_wl[k]] +
                            ([t_ra[k - 1]] if k >= 1 else []))
                def radd(e, c, thalf=thalf, ob=ob):
                    for tc in range(4):
                        i = thalf * 4 + tc
                        e.tensor_tensor(xs[i][:, ob * 512:(ob + 1) * 512],
                                        xs[i][:, ob * 512:(ob + 1) * 512],
                                        psAD[tc][:, :], OP.add)
                t_ra[k] = G("vector", radd, chain=(k == 7),
                            extra_deps=[t_mm[k]])

        def ffn(tern_toks):
            t_ag_fc = tern_toks.get("ag_fc")
            t_ag_pj = tern_toks.get("ag_pj")
            for wave in range(2):
                ldT(ln_mine, wave)
                entry1 = last_tok[0]
                wb1 = [A1(20480, 2048), A1(22528, 2048)]
                t_swl, t_mm1, t_rsq = {}, {}, {}
                for hc in range(40):
                    par = hc % 2
                    dep = [t_mm1[hc - 2]] if hc >= 2 else [entry1]
                    if t_ag_fc is not None and hc < 2:
                        dep = dep + [t_ag_fc]
                    def swl(e, c, hc=hc, buf=wb1[par]):
                        c.dma(e, buf.rearrange("p (f h) -> p f h", f=16),
                              wfcq_all[:, hc * 128:(hc + 1) * 128].rearrange(
                                  "(f p) h -> p f h", p=128))
                    t_swl[hc] = G("sync" if par == 0 else "gpsimd", swl,
                                  n_dma=1, chain=False, extra_deps=dep)
                    def mm(e, c, buf=wb1[par], par=par):
                        ps = psAD[par]
                        for f in range(16):
                            e.matmul(ps[:, :], buf[:, f * 128:(f + 1) * 128],
                                     lnT(f), start=(f == 0), stop=(f == 15))
                    t_mm1[hc] = G("tensor", mm, chain=False,
                                  extra_deps=[t_swl[hc]] +
                                  ([t_rsq[hc - 2]] if hc >= 2 else []))
                    def rsq(e, c, hc=hc, par=par):
                        ps = psAD[par]
                        scr = AR3[:, par * 512:(par + 1) * 512]
                        e.tensor_scalar(scr, ps[:, :], 0.0, None, OP.max)
                        e.drain()
                        e.tensor_tensor(A1(hc * 512, 512), scr, ps[:, :],
                                        OP.mult)
                    t_rsq[hc] = G("vector", rsq, chain=(hc == 39),
                                  extra_deps=[t_mm1[hc]])
                for ob in range(4):
                    ldk, mmk = {}, {}
                    entry2 = last_tok[0]
                    for hb in range(10):
                        par = hb % 2
                        def rwl(e, c, ob=ob, hb=hb, par=par):
                            for j in range(4):
                                hh = hb * 4 + j
                                c.dma(e, AR2[:, par * 4096 + j * 512:
                                             par * 4096 + (j + 1) * 512],
                                      wpjq_all[hh * 128:(hh + 1) * 128,
                                               ob * 512:(ob + 1) * 512])
                        dep = [mmk[hb - 2]] if hb >= 2 else [entry2]
                        if t_ag_pj is not None and wave == 0 and ob == 0 \
                                and hb < 2:
                            dep = dep + [t_ag_pj]
                        ldk[hb] = G("sync" if par == 0 else "gpsimd", rwl,
                                    n_dma=4, chain=False, extra_deps=dep)
                        def mm2(e, c, hb=hb, par=par):
                            for j in range(4):
                                hh = hb * 4 + j
                                for tc in range(4):
                                    e.matmul(psAD[tc][:, :],
                                             A1(hh * 512, 512)[:, tc * 128:
                                                               (tc + 1) * 128],
                                             AR2[:, par * 4096 + j * 512:
                                                 par * 4096 + (j + 1) * 512],
                                             start=(hh == 0), stop=(hh == 39))
                        mmk[hb] = G("tensor", mm2, chain=(hb == 9),
                                    extra_deps=[ldk[hb]] +
                                    ([mmk[hb - 1]] if hb else []))
                    def ra(e, c, wave=wave, ob=ob):
                        for tc in range(4):
                            i = wave * 4 + tc
                            e.scalar_tensor_tensor(
                                xs[i][:, ob * 512:(ob + 1) * 512],
                                psAD[tc][:, :], small[:, 15:16],
                                xs[i][:, ob * 512:(ob + 1) * 512],
                                OP.mult, OP.add)
                    G("vector", ra)

        def attn_b():
            w_entry = last_tok[0]
            t_ag = ag(ln_mine, ln_all)
            pj = new_proj_state()
            t_tb = {}
            for tb in range(4):
                t_ldt = ldT_g(ln_all, tb,
                              [t_ag] if tb == 0 else [t_tb[tb - 1]])
                for gg in range(GPC):
                    proj_job(pj, t_ldt, wk_b, gg * 128, None, None, None, tb,
                             direct_dst=A1(4096 + gg * 2048, 2048)
                             [:, tb * 512:(tb + 1) * 512], w_entry=w_entry)
                for h in range(HPC):
                    proj_job(pj, t_ldt, wq_b, h * 128, None, qt_sp, h, tb,
                             w_entry=w_entry)
                t_tb[tb] = v_job(
                    pj, t_ldt, wv_b, 0, tb,
                    tc_dst=lambda tc, tb=tb: A1((tb * 4 + tc) * 256, 256),
                    w_entry=w_entry)
            G("vector", lambda e, c: None,
              extra_deps=[t_tb[3], pj["tail"][-1],
                          pj["last_sp"][0], pj["last_sp"][1]])
            for gg in range(GPC):
                G("sync", lambda e, c: ld_pat(e, c, cos_bk, sin_bk), n_dma=2)
                gla_prep(gg)
                rope("b", A1(4096 + gg * 2048, 2048),
                     A1(4096 + gg * 2048, 2048), inplace=True)
                G("sync", lambda e, c: ld_pat(e, c, cos_bq, sin_bq), n_dma=2)
                for hh in range(4):
                    h = gg * 4 + hh
                    G("sync", lambda e, c, h=h: c.dma(
                        e, A1(16384, 2048)[:, :],
                        qt_sp[h * 128:(h + 1) * 128, :]), n_dma=1)
                    gla_head(gg, h)
                    rope("b", A1(16384, 2048), A1(18432, 2048))
                    smax_head("b", h, A1(4096 + gg * 2048, 2048),
                              A1(22528, 2048))
                    G("vector", lambda e, c: e.tensor_tensor(
                        A1(22528, 2048)[:, :], A1(22528, 2048)[:, :],
                        A1(20480, 2048)[:, :], OP.add))
                    def yst(e, c, h=h):
                        c.dma(e, yt_mine[h * 128:(h + 1) * 128, :],
                              A1(22528, 2048)[:, 0:1024])
                        c.dma(e, yt_mine[1024 + h * 128:1024 + (h + 1) * 128,
                                         :], A1(22528, 2048)[:, 1024:2048])
                    G("gpsimd", yst, n_dma=2)
            ag(yt_mine, yt_all)
            wo_resid(yt_all_full, wo_b, False)

        def gla_prep(gg):
            ktb = A1(4096 + gg * 2048, 2048)
            G("scalar", lambda e, c: e.activation(AR3[:, 2048:4096],
                                                  ktb[:, :], AF.Sigmoid))
            G("scalar", lambda e, c: e.activation(AR3[:, 0:2048],
                                                  AR3[:, 2048:4096], AF.Ln))
            G("vector", lambda e, c: e.tensor_scalar(
                AR3[:, 0:2048], AR3[:, 0:2048], -1.0, None, OP.mult))
            cur, oth = 0, 2048
            sh = 1
            while sh < 2048:
                def pd(e, c, cur=cur, oth=oth, sh=sh):
                    e.tensor_tensor(AR3[:, oth + sh:oth + 2048],
                                    AR3[:, cur + sh:cur + 2048],
                                    AR3[:, cur:cur + 2048 - sh], OP.add)
                    e.tensor_copy(AR3[:, oth:oth + sh], AR3[:, cur:cur + sh])
                G("vector", pd)
                cur, oth = oth, cur
                sh *= 2
            assert cur == 2048, "pcum must land in slot1"
            pcum = AR3[:, 2048:4096]
            ksT = A1(10240, 2048)
            entry = last_tok[0]
            K = {}

            def kS1(p): return scrq[:, p * 128:(p + 1) * 128]
            def kS2(p): return scrv[:, p * 128:(p + 1) * 128]
            def dS1(p): return scrq[:, 256 + p * 128:256 + (p + 1) * 128]
            def dS2(p): return scrv[:, 256 + p * 128:256 + (p + 1) * 128]

            for cc in range(8):
                pair = (2 * cc, 2 * cc + 1)
                for ci in pair:
                    p = ci % 2
                    dep = [K[("kex", ci - 2)]] if ci >= 2 else [entry]
                    K[("ktk", ci)] = G("tensor", lambda e, c, ci=ci, p=p,
                                       ktb=ktb: e.transpose(
                                           psT[:, p * 128:(p + 1) * 128],
                                           ktb[:, ci * 128:(ci + 1) * 128],
                                           idt[:, :]),
                                       chain=False, extra_deps=dep)
                for ci in pair:
                    p = ci % 2
                    K[("krm", ci)] = G("vector", lambda e, c, p=p:
                                       e.reduce_max(
                                           sc1[:, 3 * p:3 * p + 1],
                                           psT[:, p * 128:(p + 1) * 128],
                                           AX.X, negate=True),
                                       chain=False,
                                       extra_deps=[K[("ktk", ci)]])
                for ci in pair:
                    p = ci % 2
                    dep = [K[("krm", ci)]] + (
                        [K[("kkn", ci - 2)]] if ci >= 2 else [])
                    K[("kex", ci)] = G("scalar", lambda e, c, p=p:
                                       e.activation(
                                           kS1(p), psT[:, p * 128:
                                                       (p + 1) * 128],
                                           AF.Exp, bias=sc1[:, 3 * p:
                                                            3 * p + 1],
                                           accum_out=sc1[:, 3 * p + 1:
                                                         3 * p + 2]),
                                       chain=False, extra_deps=dep)
                for ci in pair:
                    p = ci % 2
                    dep = [K[("kex", ci)]] + (
                        [K[("kt2", ci - 2)]] if ci >= 2 else [])
                    def kn(e, c, p=p):
                        e.reciprocal(sc1[:, 3 * p + 2:3 * p + 3],
                                     sc1[:, 3 * p + 1:3 * p + 2])
                        e.drain()
                        e.tensor_scalar(kS2(p), kS1(p),
                                        sc1[:, 3 * p + 2:3 * p + 3],
                                        None, OP.mult)
                    K[("kkn", ci)] = G("vector", kn, chain=False,
                                       extra_deps=dep)
                for ci in pair:
                    p = ci % 2
                    dep = [K[("kkn", ci)]] + (
                        [K[("kcp", ci - 2)]] if ci >= 2 else [])
                    K[("kt2", ci)] = G("tensor", lambda e, c, p=p:
                                       e.transpose(
                                           psT[:, 256 + p * 128:
                                               256 + (p + 1) * 128],
                                           kS2(p), idt[:, :]),
                                       chain=False, extra_deps=dep)
                for ci in pair:
                    p = ci % 2
                    K[("kcp", ci)] = G("vector", lambda e, c, ci=ci, p=p:
                                       e.tensor_copy(
                                           ksT[:, ci * 128:(ci + 1) * 128],
                                           psT[:, 256 + p * 128:
                                               256 + (p + 1) * 128]),
                                       chain=False,
                                       extra_deps=[K[("kt2", ci)]])
                for ci in pair:
                    p = ci % 2
                    b = 6 + 4 * p
                    dep = ([K[("dge", ci - 2)], K[("dsu", ci - 2)]]
                           if ci >= 2 else [entry])
                    def gf(e, c, ci=ci, b=b, pcum=pcum):
                        c0 = ci * 128
                        if ci > 0:
                            e.tensor_scalar(sc1[:, b:b + 1],
                                            pcum[:, c0 - 1:c0],
                                            1.0 / GN, None, OP.mult)
                            e.drain()
                            e.tensor_scalar(sc1[:, b + 1:b + 2],
                                            sc1[:, b:b + 1], -1.0, None,
                                            OP.mult)
                        e.tensor_scalar(sc1[:, b + 2:b + 3],
                                        pcum[:, c0 + 127:c0 + 128],
                                        -1.0 / GN, None, OP.mult)
                    K[("dgf", ci)] = G("vector", gf, chain=False,
                                       extra_deps=dep)
                for ci in pair:
                    p = ci % 2
                    b = 6 + 4 * p
                    dep = [K[("dgf", ci)]] + (
                        [K[("dkt", ci - 2)], K[("dtx", ci - 2)]]
                        if ci >= 2 else [])
                    def ge(e, c, ci=ci, p=p, b=b, pcum=pcum):
                        c0 = ci * 128
                        if ci > 0:
                            e.activation(A1(8192, 2048)[:, c0:c0 + 128],
                                         pcum[:, c0:c0 + 128], AF.Exp,
                                         bias=sc1[:, b:b + 1],
                                         scale=-1.0 / GN)
                            e.activation(dS1(p), pcum[:, c0:c0 + 128],
                                         AF.Exp, bias=sc1[:, b + 1:b + 2],
                                         scale=1.0 / GN)
                        else:
                            e.activation(A1(8192, 2048)[:, c0:c0 + 128],
                                         pcum[:, c0:c0 + 128], AF.Exp,
                                         scale=-1.0 / GN)
                            e.activation(dS1(p), pcum[:, c0:c0 + 128],
                                         AF.Exp, scale=1.0 / GN)
                        e.activation(dS2(p), pcum[:, c0:c0 + 128], AF.Exp,
                                     bias=sc1[:, b + 2:b + 3],
                                     scale=1.0 / GN)
                        if ci > 0:
                            e.activation(sc1[:, b + 3:b + 4],
                                         pcum[:, c0 + 127:c0 + 128],
                                         AF.Exp, bias=sc1[:, b:b + 1],
                                         scale=-1.0 / GN)
                    K[("dge", ci)] = G("scalar", ge, chain=False,
                                       extra_deps=dep)
                for ci in pair:
                    p = ci % 2
                    def ktl(e, c, ci=ci, p=p, ksT=ksT):
                        c0 = ci * 128
                        e.tensor_tensor(dS2(p), ksT[:, c0:c0 + 128],
                                        dS2(p), OP.mult)
                        e.drain()
                        e.tensor_tensor(ksT[:, c0:c0 + 128],
                                        ksT[:, c0:c0 + 128], dS1(p),
                                        OP.mult)
                    K[("dkt", ci)] = G("vector", ktl, chain=False,
                                       extra_deps=[K[("dge", ci)],
                                                   K[("kcp", ci)]])
                for ci in pair:
                    p = ci % 2
                    dep = [K[("dkt", ci)]] + (
                        [K[("dcp", ci - 2)]] if ci >= 2 else [])
                    K[("dtx", ci)] = G("tensor", lambda e, c, p=p:
                                       e.transpose(
                                           psT[:, 512 + p * 128:
                                               512 + (p + 1) * 128],
                                           dS2(p), idt[:, :]),
                                       chain=False, extra_deps=dep)
                for ci in pair:
                    p = ci % 2
                    K[("dcp", ci)] = G("vector", lambda e, c, ci=ci, p=p:
                                       e.tensor_copy(
                                           A1(12288, 2048)[:, ci * 128:
                                                           (ci + 1) * 128],
                                           psT[:, 512 + p * 128:
                                               512 + (p + 1) * 128]),
                                       chain=False,
                                       extra_deps=[K[("dtx", ci)]])
                for ci in pair:
                    p = ci % 2
                    dep = [K[("dcp", ci)]] + (
                        [K[("dsu", ci - 2)]] if ci >= 2 else [])
                    def smm(e, c, ci=ci, p=p, gg=gg):
                        e.matmul(psF[:, 0:128] if p == 0
                                 else psAD[3][:, 0:128],
                                 A1(12288, 2048)[:, ci * 128:(ci + 1) * 128],
                                 A1(ci * 256, 256)[:, gg * 128:
                                                   (gg + 1) * 128],
                                 start=True, stop=True)
                    K[("dsm", ci)] = G("tensor", smm, chain=False,
                                       extra_deps=dep)
                for ci in pair:
                    p = ci % 2
                    b = 6 + 4 * p
                    dep = [K[("dsm", ci)]] + (
                        [K[("dsu", ci - 1)]] if ci >= 1 else [])
                    def supd(e, c, ci=ci, p=p, b=b):
                        c0 = ci * 128
                        if ci == 0:
                            e.tensor_copy(gsA[:, 0:128],
                                          psF[:, 0:128])
                        else:
                            e.tensor_scalar(gsA[:, ci * 128:(ci + 1) * 128],
                                            gsA[:, (ci - 1) * 128:ci * 128],
                                            sc1[:, b + 3:b + 4], None,
                                            OP.mult)
                            e.drain()
                            e.tensor_tensor(gsA[:, ci * 128:(ci + 1) * 128],
                                            gsA[:, ci * 128:(ci + 1) * 128],
                                            psF[:, 0:128] if p == 0
                                            else psAD[3][:, 0:128],
                                            OP.add)
                        e.drain()
                        e.tensor_copy(A1(14336, 2048)[:, c0:c0 + 128],
                                      gsA[:, ci * 128:(ci + 1) * 128])
                    K[("dsu", ci)] = G("vector", supd, chain=False,
                                       extra_deps=dep)
            G("vector", lambda e, c: None,
              extra_deps=[K[("dsu", 15)], K[("dsu", 14)],
                          K[("ktk", 15)], K[("ktk", 14)]])

        def gla_head(gg, h):
            qt = A1(16384, 2048)
            ktil = A1(10240, 2048)
            gstb = A1(14336, 2048)
            edec = A1(8192, 2048)
            ogla = A1(20480, 2048)
            entry = last_tok[0]
            K = {}
            def S1(p): return scrq[:, p * 128:(p + 1) * 128]
            def S2(p): return scrv[:, p * 128:(p + 1) * 128]
            def PSA(p): return psE[:, 0:128] if p == 0 \
                else psAD[2][:, 0:128]
            def PSO(p): return psF[:, 0:128] if p == 0 \
                else psAD[3][:, 0:128]

            for cc in range(8):
                pair = (2 * cc, 2 * cc + 1)
                for ci in pair:
                    p = ci % 2
                    dep = [K[("ex", ci - 2)]] if ci >= 2 else [entry]
                    K[("tq", ci)] = G("tensor", lambda e, c, ci=ci, p=p,
                                      qt=qt: e.transpose(
                                          psT[:, p * 128:(p + 1) * 128],
                                          qt[:, ci * 128:(ci + 1) * 128],
                                          idt[:, :]),
                                      chain=False, extra_deps=dep)
                for ci in pair:
                    p = ci % 2
                    b = 8 + 3 * p
                    K[("rm", ci)] = G("vector", lambda e, c, p=p, b=b:
                                      e.reduce_max(
                                          sc1[:, b:b + 1],
                                          psT[:, p * 128:(p + 1) * 128],
                                          AX.X, negate=True),
                                      chain=False,
                                      extra_deps=[K[("tq", ci)]])
                for ci in pair:
                    p = ci % 2
                    b = 8 + 3 * p
                    dep = [K[("rm", ci)]] + (
                        [K[("om", ci - 2)]] if ci >= 2 else [])
                    K[("ex", ci)] = G("scalar", lambda e, c, p=p, b=b:
                                      e.activation(
                                          S1(p), psT[:, p * 128:
                                                     (p + 1) * 128],
                                          AF.Exp, bias=sc1[:, b:b + 1],
                                          accum_out=sc1[:, b + 1:b + 2]),
                                      chain=False, extra_deps=dep)
                for ci in pair:
                    p = ci % 2
                    b = 8 + 3 * p
                    def qnr(e, c, p=p, b=b):
                        e.reciprocal(sc1[:, b + 2:b + 3],
                                     sc1[:, b + 1:b + 2])
                        e.drain()
                        e.tensor_scalar(sc1[:, b + 2:b + 3],
                                        sc1[:, b + 2:b + 3], 0.5, None,
                                        OP.mult)
                        e.drain()
                        e.tensor_scalar(S2(p), S1(p),
                                        sc1[:, b + 2:b + 3], None, OP.mult)
                    K[("qn", ci)] = G("vector", qnr, chain=False,
                                      extra_deps=[K[("ex", ci)]])
                for ci in pair:
                    p = ci % 2
                    dep = [K[("qn", ci)]] + (
                        [K[("ql", ci - 2)]] if ci >= 2 else [])
                    K[("t2", ci)] = G("tensor", lambda e, c, p=p:
                                      e.transpose(
                                          psT[:, 256 + p * 128:
                                              256 + (p + 1) * 128],
                                          S2(p), idt[:, :]),
                                      chain=False, extra_deps=dep)
                for ci in pair:
                    p = ci % 2
                    K[("ql", ci)] = G("vector", lambda e, c, ci=ci, p=p,
                                      edec=edec: e.tensor_tensor(
                                          S1(p),
                                          psT[:, 256 + p * 128:
                                              256 + (p + 1) * 128],
                                          edec[:, ci * 128:(ci + 1) * 128],
                                          OP.mult),
                                      chain=False,
                                      extra_deps=[K[("t2", ci)]])
                for ci in pair:
                    p = ci % 2
                    dep = [K[("ql", ci)]] + (
                        [K[("mk", ci - 2)]] if ci >= 2 else [])
                    K[("am", ci)] = G("tensor", lambda e, c, ci=ci, p=p,
                                      ktil=ktil: e.matmul(
                                          PSA(p),
                                          ktil[:, ci * 128:(ci + 1) * 128],
                                          S1(p), start=True, stop=True),
                                      chain=False, extra_deps=dep)
                for ci in pair:
                    p = ci % 2
                    K[("mk", ci)] = G("vector", lambda e, c, p=p:
                                      e.tensor_tensor(
                                          S2(p), PSA(p),
                                          trim[:, :], OP.mult),
                                      chain=False,
                                      extra_deps=[K[("am", ci)]])
                for ci in pair:
                    p = ci % 2
                    dep = [K[("mk", ci)]] + (
                        [K[("cp", ci - 2)]] if ci >= 2 else [])
                    def omm(e, c, ci=ci, p=p, gg=gg, gstb=gstb):
                        e.matmul(PSO(p),
                                 A1(ci * 256, 256)[:, gg * 128:
                                                   (gg + 1) * 128],
                                 S2(p), start=True, stop=(ci == 0))
                        if ci > 0:
                            e.matmul(PSO(p),
                                     gstb[:, (ci - 1) * 128:ci * 128],
                                     S1(p), start=False, stop=True)
                    K[("om", ci)] = G("tensor", omm, chain=False,
                                      extra_deps=dep)
                for ci in pair:
                    p = ci % 2
                    K[("cp", ci)] = G("vector", lambda e, c, ci=ci, p=p,
                                      ogla=ogla: e.tensor_copy(
                                          ogla[:, ci * 128:(ci + 1) * 128],
                                          PSO(p)),
                                      chain=False,
                                      extra_deps=[K[("om", ci)]])
            G("vector", lambda e, c: None,
              extra_deps=[K[("cp", 15)], K[("cp", 14)],
                          K[("tq", 15)], K[("tq", 14)]])

        t_sq = ln_phase()
        tern_toks = tern_phase(t_sq, t_bbc)
        attn_a(tern_toks)
        if N_SUBLAYERS >= 2:
            ln_phase()
            ffn(tern_toks)
        if N_SUBLAYERS >= 3:
            ln_phase()
            attn_b()
        if N_SUBLAYERS >= 4:
            ln_phase()
            ffn(tern_toks)

        def store_out(e, c):
            for i in range(NTT):
                c.dma(e, out_t[i * 128:(i + 1) * 128, :], xs[i][:, :])
        G("gpsimd", store_out, n_dma=NTT)
        G("gpsimd", lambda e, c: None)

        WAIT = [None] * len(groups)
        cnt = {k: 0 for k in sems}
        OWNER = [None] * len(groups)
        for gi, (eng, kind, n_dma, deps, emit, dsem) in enumerate(groups):
            if kind == "dma":
                key = dsem or eng
                cnt[key] += 16 * n_dma
                WAIT[gi] = (key, cnt[key])
            elif kind == "cc":
                cnt["cc"] += 1
                WAIT[gi] = ("cc", cnt["cc"])
            elif eng in ("sync", "gpsimd"):
                WAIT[gi] = (eng, cnt[eng])
            else:
                cnt[eng] += 1
                WAIT[gi] = (eng, cnt[eng])
            OWNER[gi] = eng

        byeng = {e: [gi for gi, g in enumerate(groups) if g[0] == e]
                 for e in ENGS}
        pc = {e: 0 for e in ENGS}
        done = [False] * len(groups)
        progress = True
        while progress:
            progress = False
            for e in ENGS:
                while pc[e] < len(byeng[e]):
                    gi = byeng[e][pc[e]]
                    if all(done[d] for d in groups[gi][3]):
                        done[gi] = True
                        pc[e] += 1
                        progress = True
                    else:
                        break
        stuck = [e for e in ENGS if pc[e] < len(byeng[e])]
        if stuck:
            msg = []
            for e in stuck:
                gi = byeng[e][pc[e]]
                blockers = [d for d in groups[gi][3] if not done[d]]
                msg.append(f"{e} stuck at group {gi} waiting {blockers} "
                           f"(owners {[groups[d][0] for d in blockers]})")
            raise RuntimeError("deadlock in group graph: " + "; ".join(msg))

        class EngProxy:
            def __init__(self, e):
                object.__setattr__(self, "_e", e)
                object.__setattr__(self, "last", None)
            def __getattr__(self, name):
                f = getattr(self._e, name)
                if callable(f):
                    def wrap(*a, **k):
                        r = f(*a, **k)
                        if hasattr(r, "then_inc"):
                            object.__setattr__(self, "last", r)
                        return r
                    return wrap
                return f

        class Ctx:
            def __init__(self, sem):
                self.sem = sem
                self.n = 0
            def dma(self, e, out, in_):
                e.dma_start(out=out, in_=in_).then_inc(self.sem, 16)
                self.n += 1
            def dma_t(self, e, out, in_):
                e.dma_start_transpose(out, in_).then_inc(self.sem, 16)
                self.n += 1
            def dma_nc(self, e, out, in_):
                e.dma_start(out=out, in_=in_,
                            allow_slow_non_contiguous=True
                            ).then_inc(self.sem, 16)
                self.n += 1

        with nc.Block() as block:
            def mk(eng):
                def body(e):
                    c = Ctx(sems[eng])
                    for gi, (geng, kind, n_dma, deps, emit,
                             dsem) in enumerate(groups):
                        if geng != eng:
                            continue
                        for d in deps:
                            deng, dval = WAIT[d]
                            e.wait_ge(sems[deng], dval)
                        c.sem = sems[dsem] if dsem else sems[eng]
                        n0 = c.n
                        px = EngProxy(e)
                        emit(px, c)
                        if kind == "dma":
                            assert c.n - n0 == n_dma, \
                                f"group {gi} declared {n_dma} dmas, did {c.n - n0}"
                        elif kind == "cc":
                            pass
                        elif eng in ("sync", "gpsimd"):
                            assert c.n == n0, f"group {gi} unexpected dma"
                        else:
                            assert c.n == n0, f"group {gi} unexpected dma"
                            if px.last is not None:
                                px.last.then_inc(sems[eng], 1)
                            else:
                                e.nop().then_inc(sems[eng], 1)
                return body
            for eng in ENGS:
                getattr(block, eng)(mk(eng))

    return nc


_NC_CACHE = [None]


def prepare_in_maps(inputs):
    consts = _np_consts()
    xin = np.asarray(inputs["L0_attn_a_hidden_states"], dtype=np.float32)
    w_qkv = np.asarray(inputs["L0_attn_a_w_qkv"], np.float32)
    b_qkv = np.asarray(inputs["L0_attn_a_b_qkv"], np.float32)
    w_out = np.ascontiguousarray(np.asarray(inputs["L0_attn_a_w_out"],
                                            np.float32))
    b_out = np.asarray(inputs["L0_attn_a_b_out"], np.float32)
    Wq = np.asarray(inputs["L0_attn_b_Wq"], np.float32)
    Wk = np.asarray(inputs["L0_attn_b_Wk"], np.float32)
    Wv = np.asarray(inputs["L0_attn_b_Wv"], np.float32)
    Wo = np.ascontiguousarray(np.asarray(inputs["L0_attn_b_Wo"], np.float32))
    w_fc = np.asarray(inputs["L0_ffn_w_fc"], np.float32)
    w_proj = np.asarray(inputs["L0_ffn_w_proj"], np.float32)

    qkv3 = w_qkv.reshape(HID, NH, 3 * D)
    b3 = b_qkv.reshape(NH, 3 * D)
    wfcT = np.ascontiguousarray(w_fc.T)
    wprojT = np.ascontiguousarray(w_proj.T)

    in_maps = []
    for c in range(8):
        b, p = c // 2, c % 2
        heads = list(range(HPC * p, HPC * (p + 1)))
        wqk_c = np.concatenate(
            [qkv3[:, g, 0:2 * D] for g in heads], axis=1)
        wva_c = np.concatenate(
            [qkv3[:, g, 2 * D:3 * D] for g in heads], axis=1)
        bq_c = np.concatenate([b3[g, 0:D] for g in heads])[:, None]
        bk_c = np.concatenate([b3[g, D:2 * D] for g in heads])[:, None]
        bv_c = np.concatenate([b3[g, 2 * D:3 * D] for g in heads])[:, None]
        m = {
            "x0": np.ascontiguousarray(xin[b, p * TOK:(p + 1) * TOK, :]),
            "wqk": np.ascontiguousarray(wqk_c),
            "wv_a": np.ascontiguousarray(wva_c),
            "b_q": np.ascontiguousarray(bq_c),
            "b_k": np.ascontiguousarray(bk_c),
            "b_v": np.ascontiguousarray(bv_c),
            "w_out": w_out,
            "b_out": np.ascontiguousarray(b_out[:, None]),
            "wq_b": np.ascontiguousarray(Wq[:, HPC * D * p: HPC * D * (p + 1)]),
            "wk_b": np.ascontiguousarray(Wk[:, GPC * D * p: GPC * D * (p + 1)]),
            "wv_b": np.ascontiguousarray(Wv[:, GPC * D * p: GPC * D * (p + 1)]),
            "wo_b": Wo,
            "wfc_sh": np.ascontiguousarray(wfcT[256 * c:256 * (c + 1), :]),
            "wpj_sh": np.ascontiguousarray(wprojT[640 * c:640 * (c + 1), :]),
        }
        m.update({k: consts[k] for k in
                  ("cos_a", "sin_a", "cos_bq", "sin_bq", "cos_bk", "sin_bk",
                   "perm16", "perm64", "ident", "cmask", "trimask", "constv")})
        in_maps.append(m)
    return in_maps


def kernel(**inputs):
    in_maps = prepare_in_maps(inputs)
    if _NC_CACHE[0] is None:
        _NC_CACHE[0] = build_nc()
    nc = _NC_CACHE[0]
    res = run_bass_kernel_spmd(nc, in_maps, core_ids=list(range(8)))
    out = np.zeros((B, S, HID), dtype=np.float32)
    for c in range(8):
        b, p = c // 2, c % 2
        out[b, p * TOK:(p + 1) * TOK, :] = res.results[c]["out"]
    return out
